# version 22
# baseline (speedup 1.0000x reference)
import numpy as np
import concourse.bass as bass
import concourse.mybir as mybir
from concourse.bass_utils import run_bass_kernel_spmd

F32 = mybir.dt.float32
F32R = mybir.dt.float32r
BF16 = mybir.dt.bfloat16
ALU = mybir.AluOpType
ACT = mybir.ActivationFunctionType
AX = mybir.AxisListType.X

D, DI, DBC, DH = 1024, 2048, 4096, 2816
KD, KI, KB, KH = D // 128, DI // 128, DBC // 128, DH // 128
T = 512
NSEQ, SEQ = 2, 2048
NT_SEQ = SEQ // T
NT = NSEQ * NT_SEQ
SECT_H = 11
NSEC_H = KH // SECT_H
SECT_Y = 8
NSEC_Y = KI // SECT_Y


_BIG = float(2.0 ** 50)
_MIDS = (0.25, 0.75, 1.25, 1.75, 2.5, 3.5, 5.0)
_SIGB = tuple(-float(2.0 ** 50) * float(np.nextafter(np.float32(m), np.float32(np.inf)))
              for m in _MIDS)

ENGS = ("sync", "vector", "scalar", "gpsimd", "tensor")
SELF = {"vector": "vS", "scalar": "sS", "gpsimd": "gS"}


class Buf:
    __slots__ = ("w", "rs")

    def __init__(self):
        self.w = {}
        self.rs = {}

    @staticmethod
    def _merge(d, tag):
        if tag is None:
            return
        k, v = tag
        if d.get(k, 0) < v:
            d[k] = v


class Plan:
    def __init__(self, nc):
        self.nc = nc
        self.ops = {e: [] for e in ENGS}
        self.cnt = {"vS": 0, "sS": 0, "gS": 0, "tS": 0}
        self.semkeys = ["vS", "sS", "gS", "tS"]
        for i in range(6):
            self.cnt[f"q{i}"] = 0
            self.semkeys.append(f"q{i}")
        for i in range(2):
            self.cnt[f"qs{i}"] = 0
            self.semkeys.append(f"qs{i}")
        self.dma_s_i = 0
        self.seen = {e: {} for e in ENGS}
        self.sems = {}
        self.dma_i = 0

    def _waits(self, eng, bufs_r, bufs_w, extra=()):
        tags = {}
        for b in bufs_r:
            for k, v in b.w.items():
                if tags.get(k, 0) < v:
                    tags[k] = v
        for b in bufs_w:
            for d in (b.w, b.rs):
                for k, v in d.items():
                    if tags.get(k, 0) < v:
                        tags[k] = v
        for k, v in extra:
            if tags.get(k, 0) < v:
                tags[k] = v
        out = []
        seen = self.seen[eng]
        skip = SELF.get(eng)
        for k, v in tags.items():
            if k == skip:
                continue
            if seen.get(k, 0) >= v:
                continue
            seen[k] = v
            out.append((k, v))
        return out

    def _post(self, tag, bufs_r, bufs_w):
        for b in bufs_r:
            Buf._merge(b.rs, tag)
        for b in bufs_w:
            b.w = {}
            b.rs = {}
            Buf._merge(b.w, tag)

    def op(self, eng, fn, r=(), w=(), extra=()):
        sk = SELF[eng]
        waits = self._waits(eng, r, w, extra)
        n = self.cnt[sk]
        self.cnt[sk] = n + 1

        def emit(e, fn=fn, waits=waits, n=n, sk=sk):
            for k, v in waits:
                e.wait_ge(self.sems[k], v)
            ins = fn(e)
            if n > 0:
                ins._wait_ge(self.sems[sk], n)
            ins.then_inc(self.sems[sk], 1)

        self.ops[eng].append(emit)
        tag = (sk, n + 1)
        self._post(tag, r, w)
        return tag

    def tgroup(self, fns, r=(), w=()):
        waits = self._waits("tensor", r, w)
        n = self.cnt["tS"] + 1
        self.cnt["tS"] = n

        def emit(e, fns=fns, waits=waits):
            for k, v in waits:
                e.wait_ge(self.sems[k], v)
            ins = None
            for fn in fns:
                ins = fn(e)
            ins.then_inc(self.sems["tS"], 1)

        self.ops["tensor"].append(emit)
        tag = ("tS", n)
        self._post(tag, r, w)
        return tag

    def dma_s(self, fn, r=(), w=()):
        q = f"qs{self.dma_s_i % 2}"
        self.dma_s_i += 1
        waits = self._waits("scalar", r, w)
        prev = self.cnt[q]
        self.cnt[q] = prev + 16

        def emit(e, fn=fn, waits=waits, q=q, prev=prev):
            for k, v in waits:
                e.wait_ge(self.sems[k], v)
            ins = fn(e)
            if prev > 0:
                ins._wait_ge(self.sems[q], prev)
            ins.then_inc(self.sems[q], 16)

        self.ops["scalar"].append(emit)
        tag = (q, prev + 16)
        self._post(tag, r, w)
        return tag

    def dma(self, pool, fn, r=(), w=()):
        if pool == "a":
            q = "q4"
        elif pool == "st":
            q = "q5"
        else:
            q = f"q{self.dma_i % 4}"
            self.dma_i += 1
        waits = self._waits("sync", r, w)
        prev = self.cnt[q]
        self.cnt[q] = prev + 16

        def emit(e, fn=fn, waits=waits, q=q, prev=prev):
            for k, v in waits:
                e.wait_ge(self.sems[k], v)
            ins = fn(e)
            if prev > 0:
                ins._wait_ge(self.sems[q], prev)
            ins.then_inc(self.sems[q], 16)

        self.ops["sync"].append(emit)
        tag = (q, prev + 16)
        self._post(tag, r, w)
        return tag

    @staticmethod
    def merge_bufs(dst, srcs):
        for s in srcs:
            for k, v in s.w.items():
                if dst.w.get(k, 0) < v:
                    dst.w[k] = v
            for k, v in s.rs.items():
                if dst.rs.get(k, 0) < v:
                    dst.rs[k] = v

    def sync_wait(self, tags):
        waits = self._waits("sync", (), (), extra=tags)

        def emit(e, waits=waits):
            for k, v in waits:
                e.wait_ge(self.sems[k], v)

        self.ops["sync"].append(emit)


def _build_nc():
    nc = bass.Bass("TRN2", target_bir_lowering=False)
    P = Plan(nc)

    xin = nc.dram_tensor("x", [NSEQ, D, SEQ], F32, kind="ExternalInput").ap()
    yout = nc.dram_tensor("y", [NSEQ, D, SEQ], F32, kind="ExternalOutput").ap()
    wg = nc.dram_tensor("wg", [DH, D], F32, kind="ExternalInput").ap()
    wu = nc.dram_tensor("wu", [DH, D], F32, kind="ExternalInput").ap()
    wd = nc.dram_tensor("wd", [D, DH], F32, kind="ExternalInput").ap()
    wi = nc.dram_tensor("wi", [D, DBC], F32, kind="ExternalInput").ap()
    wb = nc.dram_tensor("wb", [DI, DBC], F32, kind="ExternalInput").ap()
    wo = nc.dram_tensor("wo", [DI, D], F32, kind="ExternalInput").ap()
    bin_ = nc.dram_tensor("bin", [128, KB], F32, kind="ExternalInput").ap()
    bbc = nc.dram_tensor("bbc", [128, KB], F32, kind="ExternalInput").ap()
    bout = nc.dram_tensor("bout", [128, KD], F32, kind="ExternalInput").ap()
    cwd = nc.dram_tensor("cwd", [128, 4, KI], F32, kind="ExternalInput").ap()
    cbd = nc.dram_tensor("cbd", [128, KI], F32, kind="ExternalInput").ap()
    ldd = nc.dram_tensor("ldd", [128, KI], F32, kind="ExternalInput").ap()
    ident_d = nc.inline_tensor(np.eye(128, dtype=np.float32), name="ident").ap()

    SG = [nc.dram_tensor(f"SG{i}", [KH, 128, KD * 128], F32R, kind="Internal").ap() for i in range(2)]
    SU = [nc.dram_tensor(f"SU{i}", [KH, 128, KD * 128], F32R, kind="Internal").ap() for i in range(2)]
    SD = [nc.dram_tensor(f"SD{i}", [KD, 128, KH * 128], F32R, kind="Internal").ap() for i in range(2)]
    SI = [nc.dram_tensor(f"SI{i}", [KB, 128, KD * 128], F32R, kind="Internal").ap() for i in range(2)]
    SB = [nc.dram_tensor(f"SB{i}", [KB, 128, KI * 128], F32R, kind="Internal").ap() for i in range(2)]
    SO = [nc.dram_tensor(f"SO{i}", [KD, 128, KI * 128], F32R, kind="Internal").ap() for i in range(2)]

    A = nc.alloc_sbuf_tensor
    X = A("X", [128, KD * T], F32)
    AQ1 = A("AQ1", [128, KD * T], F32R)
    AQ2 = A("AQ2", [128, KD * T], F32R)
    BIG1 = A("BIG1", [128, KI * T], F32R)
    BIG2 = A("BIG2", [128, KI * T], F32R)
    SIGG = A("SIGG", [128, KI * T], BF16)
    CH1 = A("CH1", [128, T], F32)
    CH2 = A("CH2", [128, T], F32)
    CH3 = A("CH3", [128, T], F32)
    QT = A("QT", [128, T], F32)
    QAB = A("QAB", [128, T], F32)
    QSG = A("QSG", [128, T], F32)
    QC = A("QC", [128, T], F32)
    QD = A("QD", [128, T], F32)
    UW = A("UW", [128, T + 4], F32)
    QBM = A("QBM", [128, 176], F32)
    QSC = A("QSC", [128, 176], F32)
    QIV = A("QIV", [128, 176], F32)
    SMU = A("SMU", [1, T], F32)
    SVR = A("SVR", [1, T], F32)
    ST1 = A("ST1", [1, T], F32)
    SY0 = A("SY0", [1, T], F32)
    SIV = A("SIV", [1, T], F32)
    MUr = A("MUr", [1, T], F32R)
    IVr = A("IVr", [1, T], F32R)
    XRr = A("XRr", [128, T], F32R)
    XRr2 = A("XRr2", [128, T], F32R)
    SQF = A("SQF", [128, T], F32)
    SQR = A("SQR", [128, T], F32R)
    SQR2 = A("SQR2", [128, T], F32R)
    UH = A("UH", [128, KI * 3], F32)
    SCAR = A("SCAR", [128, KI], F32)
    ID = A("ID", [128, 128], F32)
    ONESC = A("ONESC", [128, 1], F32R)
    ONESCf = A("ONESCf", [128, 1], F32)
    ONESR = A("ONESR", [1, 128], F32R)
    ONESRf = A("ONESRf", [1, 128], F32)
    BIN = A("BIN", [128, KB], F32)
    BBC = A("BBC", [128, KB], F32)
    BOUT = A("BOUT", [128, KD], F32)
    CW = A("CW", [128, 4 * KI], F32)
    CB = A("CB", [128, KI], F32)
    LDT = A("LDT", [128, KI], F32)
    AD = A("AD", [128, KI], F32)
    SGB = A("SGB", [128, 8], F32)
    T16A = A("T16A", [128, KI], F32)
    T16B = A("T16B", [128, KI], F32)
    POOL = [A(f"POOL{i}", [128, KI * 128], F32R) for i in range(4)]

    PB = [nc.alloc_psum_tensor(f"pb{i}", [128, T], F32) for i in range(8)]

    bufs = {}

    def B(name):
        if name not in bufs:
            bufs[name] = Buf()
        return bufs[name]

    bXc = [B(f"Xc{i}") for i in range(KD)]
    bXB = B("XB")
    bAQ1, bAQ2, bBIG1, bBIG2, bSIGG = B("AQ1"), B("AQ2"), B("BIG1"), B("BIG2"), B("SIGG")
    bPT = [B("PTa"), B("PTb")]
    bW1p = [B("W1a"), B("W1b")]
    bW2p = [B("W2a"), B("W2b")]
    bCH1, bQT, bQAB, bQSG, bQC, bQD = B("CH1"), B("QT"), B("QAB"), B("QSG"), B("QC"), B("QD")
    bCH2 = B("CH2")
    bCH3 = B("CH3")
    bUW, bQBM, bQSC, bQIV, bSM = B("UW"), B("QBM"), B("QSC"), B("QIV"), B("SM")
    bMUr, bIVr, bXRr, bSQF, bSQR = B("MUr"), B("IVr"), B("XRr"), B("SQF"), B("SQR")
    bXRr2, bSQR2 = B("XRr2"), B("SQR2")
    bUH, bSCAR = B("UH"), B("SCAR")
    bPOOL = [B(f"POOL{i}") for i in range(4)]
    bPB = [B(f"pb{i}") for i in range(8)]
    bSG = [B("SG0"), B("SG1")]
    bSU = [B("SU0"), B("SU1")]
    bSD = [B("SD0"), B("SD1")]
    bSI = [B("SI0"), B("SI1")]
    bSBc = [B("SBc0"), B("SBc1")]
    bSO = [B("SO0"), B("SO1")]
    bconsts = B("consts")

    P.dma("qa", lambda e: e.dma_start(out=ID[:], in_=ident_d), w=[bconsts])
    for src, dst in ((bin_, BIN), (bbc, BBC), (bout, BOUT), (cbd, CB), (ldd, LDT)):
        P.dma("qa", lambda e, s=src, d=dst: e.dma_start(out=d[:], in_=s), w=[bconsts])
    P.dma("qa", lambda e: e.dma_start(
        out=CW[:].rearrange("p (a j) -> p a j", a=4), in_=cwd), w=[bconsts])
    for i in range(7):
        P.op("vector", lambda e, i=i: e.memset(SGB[:, i:i + 1], _SIGB[i]), w=[bconsts])
    P.op("vector", lambda e: e.memset(ONESCf[:], 1.0), w=[bconsts])
    P.op("vector", lambda e: e.memset(ONESRf[:], 1.0), w=[bconsts])
    P.op("vector", lambda e: e.tensor_copy(ONESC[:], ONESCf[:]), r=[bconsts], w=[bconsts])
    P.op("vector", lambda e: e.tensor_copy(ONESR[:], ONESRf[:]), r=[bconsts], w=[bconsts])
    P.op("scalar", lambda e: e.activation(T16A[:], LDT[:], ACT.Abs, scale=-1.0), r=[bconsts], w=[B("T16A")])
    P.op("scalar", lambda e: e.activation(T16A[:], T16A[:], ACT.Exp), r=[B("T16A")], w=[B("T16A")])
    P.op("vector", lambda e: e.tensor_single_scalar(T16A[:], T16A[:], 1.0, ALU.add), r=[B("T16A")], w=[B("T16A")])
    P.op("scalar", lambda e: e.activation(T16A[:], T16A[:], ACT.Ln), r=[B("T16A")], w=[B("T16A")])
    P.op("vector", lambda e: e.tensor_single_scalar(T16B[:], LDT[:], 0.0, ALU.max), r=[bconsts], w=[B("T16B")])
    P.op("vector", lambda e: e.tensor_add(T16A[:], T16A[:], T16B[:]), r=[B("T16A"), B("T16B")], w=[B("T16A")])
    P.op("scalar", lambda e: e.activation(AD[:], T16A[:], ACT.Exp, scale=-1.0), r=[B("T16A")], w=[bconsts])

    class Scr:
        def __init__(self, qt, qab, qsg, qc, qd, qbm, qsc, qiv,
                     bqt, bqab, bqsg, bqc, bqd, bqbm, bqsc, bqiv, qe=None, bqe=None):
            self.QT, self.QAB, self.QSG, self.QC, self.QD = qt, qab, qsg, qc, qd
            self.QBM, self.QSC, self.QIV = qbm, qsc, qiv
            self.bQT, self.bQAB, self.bQSG, self.bQC, self.bQD = bqt, bqab, bqsg, bqc, bqd
            self.bQBM, self.bQSC, self.bQIV = bqbm, bqsc, bqiv
            self.QE, self.bQE = qe, bqe

    SCR_A = Scr(QT[:], QAB[:], QSG[:], QC[:], QD[:],
                QBM[:], QSC[:], QIV[:],
                bQT, bQAB, bQSG, bQC, bQD, bQBM, bQSC, bQIV,
                qe=SQF[:], bqe=bSQF)
    SCR_B = Scr(X[:, 0:512], X[:, 512:1024], X[:, 1024:1536], X[:, 1536:2048],
                X[:, 2048:2560], X[:, 2560:2736], X[:, 2736:2912], X[:, 2912:3088],
                bXB, bXB, bXB, bXB, bXB, bXB, bXB, bXB,
                qe=X[:, 3088:3600], bqe=bXB)

    def stair_core(S, w):
        qc, qd, ab = S.QC[:, 0:w], S.QD[:, 0:w], S.QAB[:, 0:w]
        P.op("vector", lambda e: e.tensor_single_scalar(qc, ab, 0.25, ALU.is_gt),
             r=[S.bQAB], w=[S.bQC])
        for th in (0.75, 1.25, 1.75):
            P.op("vector", lambda e, th=th: e.scalar_tensor_tensor(qc, ab, th, qc, ALU.is_gt, ALU.add),
                 r=[S.bQAB], w=[S.bQC])
        P.op("vector", lambda e: e.tensor_single_scalar(qd, ab, 2.5, ALU.is_gt),
             r=[S.bQAB], w=[S.bQD])
        P.op("vector", lambda e: e.scalar_tensor_tensor(qd, ab, 3.5, qd, ALU.is_gt, ALU.add),
             r=[S.bQAB], w=[S.bQD])
        P.op("vector", lambda e: e.scalar_tensor_tensor(qc, qc, 0.5, qd, ALU.mult, ALU.add),
             r=[S.bQD], w=[S.bQC])
        P.op("vector", lambda e: e.tensor_scalar(qd, ab, 5.0, 2.0, ALU.is_gt, ALU.mult),
             r=[S.bQAB], w=[S.bQD])
        P.op("vector", lambda e: e.tensor_add(qc, qc, qd), r=[S.bQD], w=[S.bQC])

    def quant_core(S, gdim, nblk, w, src_ap=None, src_b=None, offload=False, stair="v"):
        if src_ap is None:
            src_ap, src_b = S.QT[:, 0:w], S.bQT
        s4 = src_ap.rearrange("p (g b k) -> p g b k", g=gdim, k=16)
        v4 = S.QT[:, 0:w].rearrange("p (g b k) -> p g b k", g=gdim, k=16)
        bm = S.QBM[:, 0:gdim * nblk].rearrange("p (g b) -> p g b", g=gdim)
        sc = S.QSC[:, 0:gdim * nblk]
        iv = S.QIV[:, 0:gdim * nblk]
        P.op("vector", lambda e: e.tensor_reduce(bm, s4, AX, ALU.max, apply_absolute_value=True),
             r=[src_b], w=[S.bQBM])
        P.op("vector", lambda e: e.tensor_scalar(sc, S.QBM[:, 0:gdim * nblk], 1.0 / 6.0, 1e-12,
                                                 ALU.mult, ALU.max),
             r=[S.bQBM], w=[S.bQSC])
        P.op("vector", lambda e: e.reciprocal(iv, sc), r=[S.bQSC], w=[S.bQIV])
        ivb = iv.rearrange("p (g b) -> p g b", g=gdim).unsqueeze(3).broadcast_to([128, gdim, nblk, 16])
        scb = sc.rearrange("p (g b) -> p g b", g=gdim).unsqueeze(3).broadcast_to([128, gdim, nblk, 16])
        P.op("vector", lambda e: e.tensor_tensor(v4, s4, ivb, ALU.mult), r=[S.bQIV, src_b], w=[S.bQT])
        P.op("scalar", lambda e: e.activation(S.QAB[:, 0:w], S.QT[:, 0:w], ACT.Abs),
             r=[S.bQT], w=[S.bQAB])
        P.op("scalar", lambda e: e.activation(S.QSG[:, 0:w], S.QT[:, 0:w], ACT.Sign),
             r=[S.bQT], w=[S.bQSG])
        qc, qd, qe, ab = S.QC[:, 0:w], S.QD[:, 0:w], S.QE[:, 0:w], S.QAB[:, 0:w]
        if stair == "sigmoid":
            P.op("scalar", lambda e: e.activation(qc, ab, ACT.Sigmoid, bias=SGB[:, 0:1], scale=_BIG),
                 r=[S.bQAB, bconsts], w=[S.bQC])
            P.op("scalar", lambda e: e.activation(qd, ab, ACT.Sigmoid, bias=SGB[:, 1:2], scale=_BIG),
                 r=[S.bQAB, bconsts], w=[S.bQD])
            P.op("gpsimd", lambda e: e.tensor_add(qc, qc, qd), r=[S.bQD], w=[S.bQC])
            for i in (2, 3):
                P.op("scalar", lambda e, i=i: e.activation(qd, ab, ACT.Sigmoid, bias=SGB[:, i:i + 1],
                                                           scale=_BIG),
                     r=[S.bQAB, bconsts], w=[S.bQD])
                P.op("gpsimd", lambda e: e.tensor_add(qc, qc, qd), r=[S.bQD], w=[S.bQC])
            P.op("scalar", lambda e: e.activation(qe, ab, ACT.Sigmoid, bias=SGB[:, 4:5], scale=_BIG),
                 r=[S.bQAB, bconsts], w=[S.bQE])
            P.op("scalar", lambda e: e.activation(qd, ab, ACT.Sigmoid, bias=SGB[:, 5:6], scale=_BIG),
                 r=[S.bQAB, bconsts], w=[S.bQD])
            P.op("gpsimd", lambda e: e.tensor_add(qd, qd, qe), r=[S.bQE], w=[S.bQD])
            P.op("vector", lambda e: e.scalar_tensor_tensor(qc, qc, 0.5, qd, ALU.mult, ALU.add),
                 r=[S.bQD], w=[S.bQC])
            P.op("scalar", lambda e: e.activation(qd, ab, ACT.Sigmoid, bias=SGB[:, 6:7], scale=_BIG),
                 r=[S.bQAB, bconsts], w=[S.bQD])
            P.op("vector", lambda e: e.scalar_tensor_tensor(qc, qd, 2.0, qc, ALU.mult, ALU.add),
                 r=[S.bQD], w=[S.bQC])
        else:
            P.op("vector", lambda e: e.tensor_single_scalar(qc, ab, 0.25, ALU.is_gt),
                 r=[S.bQAB], w=[S.bQC])
            for th in (0.75, 1.25, 1.75):
                P.op("vector", lambda e, th=th: e.scalar_tensor_tensor(qc, ab, th, qc, ALU.is_gt, ALU.add),
                     r=[S.bQAB], w=[S.bQC])
            P.op("vector", lambda e: e.tensor_single_scalar(qd, ab, 2.5, ALU.is_gt),
                 r=[S.bQAB], w=[S.bQD])
            P.op("vector", lambda e: e.scalar_tensor_tensor(qd, ab, 3.5, qd, ALU.is_gt, ALU.add),
                 r=[S.bQAB], w=[S.bQD])
            P.op("vector", lambda e: e.tensor_scalar(qe, ab, 5.0, 2.0, ALU.is_gt, ALU.mult),
                 r=[S.bQAB], w=[S.bQE])
            P.op("vector", lambda e: e.scalar_tensor_tensor(qc, qc, 0.5, qd, ALU.mult, ALU.add),
                 r=[S.bQD], w=[S.bQC])
            eng_o2 = "gpsimd" if offload else "vector"
            P.op(eng_o2, lambda e: e.tensor_add(qc, qc, qe), r=[S.bQE], w=[S.bQC])
        eng_o = "gpsimd" if offload else "vector"
        P.op(eng_o, lambda e: e.tensor_mul(qc, qc, S.QSG[:, 0:w]), r=[S.bQSG], w=[S.bQC])
        qc4 = S.QC[:, 0:w].rearrange("p (g b k) -> p g b k", g=gdim, k=16)
        P.op("vector", lambda e: e.tensor_tensor(v4, qc4, scb, ALU.mult), r=[S.bQC, S.bQSC], w=[S.bQT])

    def quant_front(src_sb, src_buf, pbA):
        bA = bPB[pbA]
        P.tgroup([(lambda e, g=g, pbA=pbA, src_sb=src_sb: e.transpose(
            PB[pbA][:, g * 128:(g + 1) * 128], src_sb[:, g * 128:(g + 1) * 128], ID[:]))
                  for g in range(4)], r=[src_buf, bconsts], w=[bA])

    def quant_back(q1_ap, q2_ap, dst1_buf, dst2_buf, nblk, pbB, pbB_src=0, offload=False):
        bB = bPB[pbB]
        quant_core(SCR_A, 4, nblk, T, src_ap=PB[pbB_src][:], src_b=bPB[pbB_src], offload=offload)
        P.tgroup([(lambda e, g=g, pbB=pbB: e.transpose(
            PB[pbB][:, g * 128:(g + 1) * 128], QT[:, g * 128:(g + 1) * 128], ID[:]))
                  for g in range(4)], r=[bQT, bconsts], w=[bB])
        P.op("vector", lambda e, pbB=pbB: e.tensor_copy(q1_ap, PB[pbB][:]), r=[bB], w=[dst1_buf])
        P.op("vector", lambda e, pbB=pbB: e.tensor_tensor(q2_ap, PB[pbB][:], q1_ap.bitcast(F32),
                                                          ALU.subtract),
             r=[bB, dst1_buf], w=[dst2_buf])

    def quant_tile(src_sb, src_buf, q1_ap, q2_ap, dst1_buf, dst2_buf, nblk, pbA, pbB, offload=False):
        quant_front(src_sb, src_buf, pbA)
        quant_back(q1_ap, q2_ap, dst1_buf, dst2_buf, nblk, pbB, pbB_src=pbA, offload=offload)

    def prep_ffn_weight(src, s1, s2, bs1, bs2, nrows_chunks, kfree):
        pts = [(AQ2[:, 0:512], AQ2[:, 512:1024]), (AQ2[:, 1024:1536], AQ2[:, 1536:2048])]
        pieces = [(j, c0) for j in range(nrows_chunks) for c0 in range(0, kfree, 512)]
        pend = None

        def flush(pend):
            (j, c0, wdt, pt1, pt2, bpt, pbi) = pend
            P.op("vector", lambda e, wdt=wdt, pt1=pt1, pbi=pbi: e.tensor_copy(
                pt1[:, 0:wdt], PB[pbi][:, 0:wdt]), r=[bPB[pbi]], w=[bpt])
            P.op("vector", lambda e, wdt=wdt, pt1=pt1, pt2=pt2, pbi=pbi: e.tensor_tensor(
                pt2[:, 0:wdt], PB[pbi][:, 0:wdt], pt1[:, 0:wdt].bitcast(F32), ALU.subtract),
                r=[bPB[pbi]], w=[bpt])
            P.dma_s(lambda e, j=j, c0=c0, wdt=wdt, pt1=pt1: e.dma_start(
                out=s1[j, :, c0:c0 + wdt], in_=pt1[:, 0:wdt]), r=[bpt], w=[bs1])
            P.dma_s(lambda e, j=j, c0=c0, wdt=wdt, pt2=pt2: e.dma_start(
                out=s2[j, :, c0:c0 + wdt], in_=pt2[:, 0:wdt]), r=[bpt], w=[bs2])

        for pi, (j, c0) in enumerate(pieces):
            S = SCR_A if pi % 2 == 0 else SCR_B
            pt1, pt2 = pts[pi % 2]
            bpt = bPT[pi % 2]
            pbi = pi % 2
            cw_ = min(512, kfree - c0)
            nb = cw_ // 16
            nk = cw_ // 128
            P.dma("qa", lambda e, j=j, c0=c0, cw_=cw_, S=S: e.dma_start(
                out=S.QT[:, 0:cw_], in_=src[j * 128:(j + 1) * 128, c0:c0 + cw_]),
                r=[S.bQT], w=[S.bQT])
            quant_core(S, 1, nb, cw_)
            P.tgroup([(lambda e, kk=kk, S=S, pbi=pbi: e.transpose(
                PB[pbi][:, kk * 128:(kk + 1) * 128], S.QT[:, kk * 128:(kk + 1) * 128], ID[:]))
                      for kk in range(nk)], r=[S.bQT, bconsts], w=[bPB[pbi]])
            if pend is not None:
                flush(pend)
            pend = (j, c0, nk * 128, pt1, pt2, bpt, pbi)
        flush(pend)

    prep_ffn_weight(wg, SG[0], SG[1], bSG[0], bSG[1], KH, D)
    prep_ffn_weight(wu, SU[0], SU[1], bSU[0], bSU[1], KH, D)
    prep_ffn_weight(wd, SD[0], SD[1], bSD[0], bSD[1], KD, DH)

    def prep_mm_weight(src, s1, s2, bs1, bs2, kchunks, m, deferred=False):
        mb = m // 128
        pieces = [(k, b0) for k in range(kchunks) for b0 in range(0, mb, 4)]
        for pi, (k, b0) in enumerate(pieces):
            if deferred:
                S = SCR_A
                w1 = BIG1[:, 11 * T:11 * T + 512]
                w2 = BIG2[:, 11 * T:11 * T + 512]
                bw1, bw2 = bBIG1, bBIG2
            else:
                S = SCR_A if pi % 2 == 0 else SCR_B
                w1 = AQ1[:, 0:512] if pi % 2 == 0 else AQ1[:, 512:1024]
                w2 = AQ2[:, 0:512] if pi % 2 == 0 else AQ2[:, 512:1024]
                bw1, bw2 = bW1p[pi % 2], bW2p[pi % 2]
            bw = min(4, mb - b0)
            cw_ = bw * 128
            P.dma("qa", lambda e, k=k, b0=b0, cw_=cw_, S=S: e.dma_start(
                out=S.QT[:, 0:cw_], in_=src[k * 128:(k + 1) * 128, b0 * 128:b0 * 128 + cw_]),
                r=[S.bQT], w=[S.bQT])
            P.op("vector", lambda e, cw_=cw_, S=S, w1=w1: e.tensor_copy(w1[:, 0:cw_], S.QT[:, 0:cw_]),
                 r=[S.bQT], w=[bw1])
            P.op("gpsimd", lambda e, cw_=cw_, S=S, w1=w1, w2=w2: e.tensor_tensor(
                w2[:, 0:cw_], S.QT[:, 0:cw_], w1[:, 0:cw_].bitcast(F32), ALU.subtract),
                r=[S.bQT, bw1], w=[bw2])
            for ws, bws, sd, bd in ((w1, bw1, s1, bs1), (w2, bw2, s2, bs2)):
                P.dma_s(lambda e, ws=ws, sd=sd, k=k, b0=b0, bw=bw, cw_=cw_: e.dma_start(
                    out=sd[b0:b0 + bw, :, k * 128:(k + 1) * 128].transpose([1, 0, 2]),
                    in_=ws[:, 0:cw_].rearrange("p (b q) -> p b q", b=bw)),
                    r=[bws], w=[bd])

    Plan.merge_bufs(bW2p[0], bPT)
    Plan.merge_bufs(bW2p[1], bPT)
    Plan.merge_bufs(bW1p[0], bPT)
    Plan.merge_bufs(bW1p[1], bPT)
    prep_mm_weight(wi, SI[0], SI[1], bSI[0], bSI[1], KD, DBC)


    Plan.merge_bufs(bAQ1, bW1p)
    Plan.merge_bufs(bAQ2, bPT + bW2p)
    for _b in bXc:
        Plan.merge_bufs(_b, [bXB])

    def layernorm_prep():
        for ch in range(KD):
            xc = X[:, ch * T:(ch + 1) * T]
            xr, bxr = (XRr, bXRr) if ch % 2 == 0 else (XRr2, bXRr2)
            sq, bsq = (SQR, bSQR) if ch % 2 == 0 else (SQR2, bSQR2)
            P.op("vector", lambda e, xc=xc, xr=xr: e.tensor_copy(xr[:], xc), r=[bXc[ch]], w=[bxr])
            P.tgroup([(lambda e, ch=ch, xr=xr: e.matmul(PB[6][0:1, :], ONESC[:], xr[:],
                                                        start=(ch == 0), stop=(ch == KD - 1),
                                                        skip_group_check=True))],
                     r=[bxr, bconsts], w=[bPB[6]])
            P.op("vector", lambda e, xc=xc: e.tensor_tensor(SQF[:], xc, xc, ALU.mult), r=[bXc[ch]], w=[bSQF])
            P.op("gpsimd", lambda e, sq=sq: e.tensor_copy(sq[:], SQF[:]), r=[bSQF], w=[bsq])
            P.tgroup([(lambda e, ch=ch, sq=sq: e.matmul(PB[7][0:1, :], ONESC[:], sq[:],
                                                        start=(ch == 0), stop=(ch == KD - 1),
                                                        skip_group_check=True))],
                     r=[bsq, bconsts], w=[bPB[7]])
        ln_smalls_and_bcasts()

    def ln_smalls_and_bcasts():
        mu, var, t1, y0, ivs = SMU[:], SVR[:], ST1[:], SY0[:], SIV[:]
        P.op("vector", lambda e: e.tensor_single_scalar(mu, PB[6][0:1, :], 1.0 / D, ALU.mult),
             r=[bPB[6]], w=[bSM])
        P.op("vector", lambda e: e.tensor_single_scalar(var, PB[7][0:1, :], 1.0 / D, ALU.mult),
             r=[bPB[7]], w=[bSM])
        P.op("vector", lambda e: e.tensor_mul(t1, mu, mu), r=[bSM], w=[bSM])
        P.op("vector", lambda e: e.tensor_sub(var, var, t1), r=[bSM], w=[bSM])
        P.op("vector", lambda e: e.tensor_single_scalar(var, var, 1e-5, ALU.add), r=[bSM], w=[bSM])
        P.op("vector", lambda e: e.reciprocal(t1, var), r=[bSM], w=[bSM])
        P.op("scalar", lambda e: e.activation(y0, t1, ACT.Sqrt), r=[bSM], w=[bSM])
        P.op("vector", lambda e: e.tensor_mul(ivs, y0, y0), r=[bSM], w=[bSM])
        P.op("vector", lambda e: e.tensor_mul(ivs, ivs, var), r=[bSM], w=[bSM])
        P.op("vector", lambda e: e.tensor_scalar(ivs, ivs, -0.5, 1.5, ALU.mult, ALU.add), r=[bSM], w=[bSM])
        P.op("vector", lambda e: e.tensor_mul(ivs, ivs, y0), r=[bSM], w=[bSM])
        P.op("vector", lambda e: e.tensor_copy(MUr[:], mu), r=[bSM], w=[bMUr])
        P.op("vector", lambda e: e.tensor_copy(IVr[:], ivs), r=[bSM], w=[bIVr])
        P.tgroup([(lambda e: e.matmul(PB[4][:], ONESR[:], MUr[:], start=True, stop=True))],
                 r=[bMUr, bconsts], w=[bPB[4]])
        P.tgroup([(lambda e: e.matmul(PB[5][:], ONESR[:], IVr[:], start=True, stop=True))],
                 r=[bIVr, bconsts], w=[bPB[5]])

    def ln_chunk(ch):
        xc = X[:, ch * T:(ch + 1) * T]
        P.op("vector", lambda e, xc=xc: e.scalar_tensor_tensor(CH1[:], xc, 1.0, PB[4][:],
                                                               ALU.bypass, ALU.subtract),
             r=[bXc[ch], bPB[4]], w=[bCH1])
        P.op("vector", lambda e: e.tensor_tensor(CH1[:], CH1[:], PB[5][:], ALU.mult),
             r=[bPB[5]], w=[bCH1])

    qwkey_f = "qw1"

    pool_i = [0]

    def next_slot():
        i = pool_i[0] % 4
        pool_i[0] += 1
        return i


    def make_stream_pieces(s2, c2):
        pieces = []

        def stat_piece(ch):
            def go():
                P.dma("st", lambda e: e.dma_start(
                    out=CH3[:], in_=xin[s2, ch * 128:(ch + 1) * 128, c2 * T:(c2 + 1) * T]),
                    r=[bCH3], w=[bCH3])
                xr, bxr = (XRr, bXRr) if ch % 2 == 0 else (XRr2, bXRr2)
                sq, bsq = (SQR, bSQR) if ch % 2 == 0 else (SQR2, bSQR2)
                P.op("gpsimd", lambda e, xr=xr: e.tensor_copy(xr[:], CH3[:]), r=[bCH3], w=[bxr])
                P.tgroup([(lambda e, ch=ch, xr=xr: e.matmul(PB[6][0:1, :], ONESC[:], xr[:],
                                                            start=(ch == 0), stop=(ch == KD - 1),
                                                            skip_group_check=True))],
                         r=[bxr, bconsts], w=[bPB[6]])
                P.op("gpsimd", lambda e: e.tensor_tensor(SQF[:], CH3[:], CH3[:], ALU.mult),
                     r=[bCH3], w=[bSQF])
                P.op("gpsimd", lambda e, sq=sq: e.tensor_copy(sq[:], SQF[:]), r=[bSQF], w=[bsq])
                P.tgroup([(lambda e, ch=ch, sq=sq: e.matmul(PB[7][0:1, :], ONESC[:], sq[:],
                                                            start=(ch == 0), stop=(ch == KD - 1),
                                                            skip_group_check=True))],
                         r=[bsq, bconsts], w=[bPB[7]])
            return go

        def smalls_piece():
            def go():
                ln_smalls_and_bcasts()
            return go

        def quant_piece(ch):
            def go():
                P.dma("st", lambda e: e.dma_start(
                    out=CH3[:], in_=xin[s2, ch * 128:(ch + 1) * 128, c2 * T:(c2 + 1) * T]),
                    r=[bCH3], w=[bCH3])
                P.op("vector", lambda e: e.scalar_tensor_tensor(CH1[:], CH3[:], 1.0, PB[4][:],
                                                                ALU.bypass, ALU.subtract),
                     r=[bCH3, bPB[4]], w=[bCH1])
                P.op("vector", lambda e: e.tensor_tensor(CH1[:], CH1[:], PB[5][:], ALU.mult),
                     r=[bPB[5]], w=[bCH1])
                quant_tile(CH1, bCH1, AQ1[:, ch * T:(ch + 1) * T], AQ2[:, ch * T:(ch + 1) * T],
                           bAQ1, bAQ2, KD, 0, 1, offload=True)
            return go

        for ch in range(KD):
            pieces.append(stat_piece(ch))
        pieces.append(smalls_piece())
        for ch in range(KD):
            pieces.append(quant_piece(ch))
        return pieces

    def ffn_sublayer(prequant=False, stream_next=None):
        if not prequant:
            layernorm_prep()
        if not prequant:
            for ch in range(KD):
                ln_chunk(ch)
                quant_tile(CH1, bCH1, AQ1[:, ch * T:(ch + 1) * T], AQ2[:, ch * T:(ch + 1) * T],
                           bAQ1, bAQ2, KD, 0, 1)
        stream_pieces = list(stream_next) if stream_next else []
        CHP = [(CH1, bCH1), (CH2, bCH2), (CH3, bCH3), (UW[:, 0:T], bUW)]
        for s in range(NSEC_H):
            ms = list(range(s * SECT_H, (s + 1) * SECT_H))
            prs = [ms[i:i + 2] for i in range(0, len(ms), 2)]
            pend = []
            chp_i = [0]

            def next_ch():
                c = CHP[chp_i[0] % 4]
                chp_i[0] += 1
                return c

            def flush_front(pend):
                for (ci, chs, chb, pbA, pbB) in pend:
                    quant_front(chs, chb, pbA)

            def flush_back(pend):
                for (ci, chs, chb, pbA, pbB) in pend:
                    quant_back(BIG1[:, ci * T:(ci + 1) * T], BIG2[:, ci * T:(ci + 1) * T],
                               bBIG1, bBIG2, KD, pbB, pbB_src=pbA)

            for pr in prs:
                if pend:
                    flush_front(pend)
                slots = {}
                for q, mb in enumerate(pr):
                    sl_a, sl_b = next_slot(), next_slot()
                    pg1, pu1 = POOL[sl_a][:, 0:KD * 128], POOL[sl_a][:, KD * 128:2 * KD * 128]
                    pg2, pu2 = POOL[sl_b][:, 0:KD * 128], POOL[sl_b][:, KD * 128:2 * KD * 128]
                    P.dma(qwkey_f, lambda e, mb=mb, d=pg1: e.dma_start(out=d, in_=SG[0][mb]), r=[bSG[0]], w=[bPOOL[sl_a]])
                    P.dma(qwkey_f, lambda e, mb=mb, d=pu1: e.dma_start(out=d, in_=SU[0][mb]), r=[bSU[0]], w=[bPOOL[sl_a]])
                    P.dma(qwkey_f, lambda e, mb=mb, d=pg2: e.dma_start(out=d, in_=SG[1][mb]), r=[bSG[1]], w=[bPOOL[sl_b]])
                    P.dma(qwkey_f, lambda e, mb=mb, d=pu2: e.dma_start(out=d, in_=SU[1][mb]), r=[bSU[1]], w=[bPOOL[sl_b]])
                    slots[mb] = (sl_a, sl_b, pg1, pu1, pg2, pu2)
                for q, mb in enumerate(pr):
                    sl_a, sl_b, pg1, pu1, pg2, pu2 = slots[mb]
                    gb, ub = 2 + 2 * q, 3 + 2 * q
                    for (p1, p2, pbi) in ((pg1, pg2, gb), (pu1, pu2, ub)):
                        fns = []
                        for k in range(KD):
                            st1 = p1[:, k * 128:(k + 1) * 128]
                            st2 = p2[:, k * 128:(k + 1) * 128]
                            m1 = AQ1[:, k * T:(k + 1) * T]
                            m2 = AQ2[:, k * T:(k + 1) * T]
                            fns.append(lambda e, st1=st1, m1=m1, k=k, pbi=pbi: e.matmul(
                                PB[pbi][:], st1, m1, start=(k == 0), stop=False))
                            fns.append(lambda e, st1=st1, m2=m2, pbi=pbi: e.matmul(
                                PB[pbi][:], st1, m2, start=False, stop=False))
                            fns.append(lambda e, st2=st2, m1=m1, k=k, pbi=pbi: e.matmul(
                                PB[pbi][:], st2, m1, start=False, stop=(k == KD - 1)))
                        P.tgroup(fns, r=[bAQ1, bAQ2, bPOOL[sl_a], bPOOL[sl_b]], w=[bPB[pbi]])
                if pend:
                    flush_back(pend)
                pend = []
                for q, mb in enumerate(pr):
                    gb, ub = 2 + 2 * q, 3 + 2 * q
                    chs, chb = next_ch()
                    P.op("scalar", lambda e, chs=chs, gb=gb: e.activation(chs[:], PB[gb][:], ACT.Sigmoid),
                         r=[bPB[gb]], w=[chb])
                    P.op("vector", lambda e, chs=chs, gb=gb: e.tensor_tensor(chs[:], chs[:], PB[gb][:],
                                                                             ALU.mult),
                         r=[bPB[gb]], w=[chb])
                    P.op("vector", lambda e, chs=chs, ub=ub: e.tensor_tensor(chs[:], chs[:], PB[ub][:],
                                                                             ALU.mult),
                         r=[bPB[ub]], w=[chb])
                    ci = mb - s * SECT_H
                    pend.append((ci, chs, chb, 0 if q == 0 else 6, 1 if q == 0 else 7))
            flush_front(pend)
            flush_back(pend)
            for mo in range(KD):
                sl = next_slot()
                sl2 = next_slot()
                pd1 = POOL[sl][:, 0:SECT_H * 128]
                pd2 = POOL[sl2][:, 0:SECT_H * 128]
                c0 = s * SECT_H * 128
                P.dma(qwkey_f, lambda e, mo=mo, c0=c0, d=pd1: e.dma_start(
                    out=d, in_=SD[0][mo, :, c0:c0 + SECT_H * 128]), r=[bSD[0]], w=[bPOOL[sl]])
                P.dma(qwkey_f, lambda e, mo=mo, c0=c0, d=pd2: e.dma_start(
                    out=d, in_=SD[1][mo, :, c0:c0 + SECT_H * 128]), r=[bSD[1]], w=[bPOOL[sl2]])
                pbi = 2 + (mo % 2)
                fns = []
                for ci in range(SECT_H):
                    st1 = pd1[:, ci * 128:(ci + 1) * 128]
                    st2 = pd2[:, ci * 128:(ci + 1) * 128]
                    m1 = BIG1[:, ci * T:(ci + 1) * T]
                    m2 = BIG2[:, ci * T:(ci + 1) * T]
                    fns.append(lambda e, st1=st1, m1=m1, ci=ci, pbi=pbi: e.matmul(
                        PB[pbi][:], st1, m1, start=(ci == 0), stop=False))
                    fns.append(lambda e, st1=st1, m2=m2, pbi=pbi: e.matmul(
                        PB[pbi][:], st1, m2, start=False, stop=False))
                    fns.append(lambda e, st2=st2, m1=m1, ci=ci, pbi=pbi: e.matmul(
                        PB[pbi][:], st2, m1, start=False, stop=(ci == SECT_H - 1)))
                P.tgroup(fns, r=[bBIG1, bBIG2, bPOOL[sl], bPOOL[sl2]], w=[bPB[pbi]])
                xc = X[:, mo * T:(mo + 1) * T]
                P.op("vector", lambda e, xc=xc, pbi=pbi: e.scalar_tensor_tensor(
                    xc, PB[pbi][:], 0.5, xc, ALU.mult, ALU.add), r=[bPB[pbi]], w=[bXc[mo]])
                if stream_pieces:
                    stream_pieces.pop(0)()
        while stream_pieces:
            stream_pieces.pop(0)()

    def mamba_sublayer(c_pos):
        layernorm_prep()
        for ch in range(KD):
            ln_chunk(ch)
            P.op("vector", lambda e, ch=ch: e.tensor_copy(AQ1[:, ch * T:(ch + 1) * T], CH1[:]),
                 r=[bCH1], w=[bAQ1])
            P.op("gpsimd", lambda e, ch=ch: e.tensor_tensor(AQ2[:, ch * T:(ch + 1) * T], CH1[:],
                                                            AQ1[:, ch * T:(ch + 1) * T].bitcast(F32),
                                                            ALU.subtract),
                 r=[bCH1, bAQ1], w=[bAQ2])
        for j in range(KI):
            sl_a, sl_b = next_slot(), next_slot()
            piu1, piu2 = POOL[sl_a][:, 0:KD * 128], POOL[sl_a][:, KD * 128:2 * KD * 128]
            pig1, pig2 = POOL[sl_b][:, 0:KD * 128], POOL[sl_b][:, KD * 128:2 * KD * 128]
            P.dma("qw2", lambda e, j=j, d=piu1: e.dma_start(out=d, in_=SI[0][j]), r=[bSI[0]], w=[bPOOL[sl_a]])
            P.dma("qw2", lambda e, j=j, d=piu2: e.dma_start(out=d, in_=SI[1][j]), r=[bSI[1]], w=[bPOOL[sl_a]])
            P.dma("qw2", lambda e, j=j, d=pig1: e.dma_start(out=d, in_=SI[0][KI + j]), r=[bSI[0]], w=[bPOOL[sl_b]])
            P.dma("qw2", lambda e, j=j, d=pig2: e.dma_start(out=d, in_=SI[1][KI + j]), r=[bSI[1]], w=[bPOOL[sl_b]])
            ub_, gb_ = 2 + 2 * (j % 2), 3 + 2 * (j % 2)
            for (p1, p2, pbi) in ((piu1, piu2, ub_), (pig1, pig2, gb_)):
                fns = []
                for k in range(KD):
                    st1 = p1[:, k * 128:(k + 1) * 128]
                    st2 = p2[:, k * 128:(k + 1) * 128]
                    m1 = AQ1[:, k * T:(k + 1) * T]
                    m2 = AQ2[:, k * T:(k + 1) * T]
                    fns.append(lambda e, st1=st1, m1=m1, k=k, pbi=pbi: e.matmul(
                        PB[pbi][:], st1, m1, start=(k == 0), stop=False))
                    fns.append(lambda e, st1=st1, m2=m2, pbi=pbi: e.matmul(
                        PB[pbi][:], st1, m2, start=False, stop=False))
                    fns.append(lambda e, st2=st2, m1=m1, k=k, pbi=pbi: e.matmul(
                        PB[pbi][:], st2, m1, start=False, stop=(k == KD - 1)))
                P.tgroup(fns, r=[bAQ1, bAQ2, bPOOL[sl_a], bPOOL[sl_b]], w=[bPB[pbi]])
            if c_pos == 0:
                P.op("vector", lambda e: e.memset(UW[:, 0:3], 0.0), w=[bUW])
            else:
                P.op("vector", lambda e, j=j: e.tensor_copy(UW[:, 0:3], UH[:, j * 3:j * 3 + 3]),
                     r=[bUH], w=[bUW])
            P.op("vector", lambda e, j=j, ub_=ub_: e.tensor_scalar(UW[:, 3:3 + T], PB[ub_][:],
                                                          BIN[:, j:j + 1], None, ALU.add),
                 r=[bPB[ub_], bconsts], w=[bUW])
            P.op("vector", lambda e, j=j: e.tensor_copy(UH[:, j * 3:j * 3 + 3], UW[:, T:T + 3]),
                 r=[bUW], w=[bUH])
            P.op("vector", lambda e, j=j: e.tensor_scalar(QC[:], UW[:, 0:T], CW[:, 0 * KI + j:0 * KI + j + 1],
                                                          None, ALU.mult),
                 r=[bUW, bconsts], w=[bQC])
            P.op("vector", lambda e, j=j: e.scalar_tensor_tensor(QD[:], UW[:, 1:1 + T],
                                                                 CW[:, 1 * KI + j:1 * KI + j + 1],
                                                                 QC[:], ALU.mult, ALU.add),
                 r=[bUW, bQC, bconsts], w=[bQD])
            P.op("vector", lambda e, j=j: e.scalar_tensor_tensor(QC[:], UW[:, 2:2 + T],
                                                                 CW[:, 2 * KI + j:2 * KI + j + 1],
                                                                 QD[:], ALU.mult, ALU.add),
                 r=[bUW, bQD, bconsts], w=[bQC])
            P.op("vector", lambda e, j=j: e.scalar_tensor_tensor(QD[:], UW[:, 3:3 + T],
                                                                 CW[:, 3 * KI + j:3 * KI + j + 1],
                                                                 QC[:], ALU.mult, ALU.add),
                 r=[bUW, bQC, bconsts], w=[bQD])
            P.op("vector", lambda e, j=j: e.tensor_scalar(QD[:], QD[:], CB[:, j:j + 1], None, ALU.add),
                 r=[bconsts], w=[bQD])
            P.op("scalar", lambda e: e.activation(QT[:], QD[:], ACT.Sigmoid),
                 r=[bQD], w=[bQT])
            P.op("vector", lambda e: e.tensor_mul(QT[:], QT[:], QD[:]),
                 r=[bQD], w=[bQT])
            P.op("vector", lambda e, j=j: e.tensor_copy(BIG1[:, j * T:(j + 1) * T], QT[:]),
                 r=[bQT], w=[bBIG1])
            P.op("gpsimd", lambda e, j=j: e.tensor_tensor(BIG2[:, j * T:(j + 1) * T], QT[:],
                                                          BIG1[:, j * T:(j + 1) * T].bitcast(F32),
                                                          ALU.subtract),
                 r=[bQT, bBIG1], w=[bBIG2])
            P.op("scalar", lambda e, j=j, gb_=gb_: e.activation(SIGG[:, j * T:(j + 1) * T], PB[gb_][:],
                                                       ACT.Sigmoid, bias=BIN[:, KI + j:KI + j + 1]),
                 r=[bPB[gb_], bconsts], w=[bSIGG])

        for j in range(KI):
            bb_, cb_ = 2 + 2 * (j % 2), 3 + 2 * (j % 2)
            sl_a, sl_b = next_slot(), next_slot()
            pb1s, pb2s = POOL[sl_a][:], POOL[sl_b][:]
            P.dma("qw2", lambda e, j=j, d=pb1s: e.dma_start(out=d, in_=SB[0][j]), r=[bSBc[0]], w=[bPOOL[sl_a]])
            P.dma("qw2", lambda e, j=j, d=pb2s: e.dma_start(out=d, in_=SB[1][j]), r=[bSBc[1]], w=[bPOOL[sl_b]])
            fns = []
            for k in range(KI):
                st1 = pb1s[:, k * 128:(k + 1) * 128]
                st2 = pb2s[:, k * 128:(k + 1) * 128]
                m1 = BIG1[:, k * T:(k + 1) * T]
                m2 = BIG2[:, k * T:(k + 1) * T]
                fns.append(lambda e, st1=st1, m1=m1, k=k, bb_=bb_: e.matmul(
                    PB[bb_][:], st1, m1, start=(k == 0), stop=False))
                fns.append(lambda e, st1=st1, m2=m2, bb_=bb_: e.matmul(
                    PB[bb_][:], st1, m2, start=False, stop=False))
                fns.append(lambda e, st2=st2, m1=m1, k=k, bb_=bb_: e.matmul(
                    PB[bb_][:], st2, m1, start=False, stop=(k == KI - 1)))
            P.tgroup(fns, r=[bBIG1, bBIG2, bPOOL[sl_a], bPOOL[sl_b]], w=[bPB[bb_]])
            sl_c, sl_d = next_slot(), next_slot()
            pc1s, pc2s = POOL[sl_c][:], POOL[sl_d][:]
            P.dma("qw2", lambda e, j=j, d=pc1s: e.dma_start(out=d, in_=SB[0][KI + j]), r=[bSBc[0]], w=[bPOOL[sl_c]])
            P.dma("qw2", lambda e, j=j, d=pc2s: e.dma_start(out=d, in_=SB[1][KI + j]), r=[bSBc[1]], w=[bPOOL[sl_d]])
            fns = []
            for k in range(KI):
                st1 = pc1s[:, k * 128:(k + 1) * 128]
                st2 = pc2s[:, k * 128:(k + 1) * 128]
                m1 = BIG1[:, k * T:(k + 1) * T]
                m2 = BIG2[:, k * T:(k + 1) * T]
                fns.append(lambda e, st1=st1, m1=m1, k=k, cb_=cb_: e.matmul(
                    PB[cb_][:], st1, m1, start=(k == 0), stop=False))
                fns.append(lambda e, st1=st1, m2=m2, cb_=cb_: e.matmul(
                    PB[cb_][:], st1, m2, start=False, stop=False))
                fns.append(lambda e, st2=st2, m1=m1, k=k, cb_=cb_: e.matmul(
                    PB[cb_][:], st2, m1, start=False, stop=(k == KI - 1)))
            P.tgroup(fns, r=[bBIG1, bBIG2, bPOOL[sl_c], bPOOL[sl_d]], w=[bPB[cb_]])
            P.op("vector", lambda e, j=j, bb_=bb_: e.tensor_scalar(QT[:], PB[bb_][:], BBC[:, j:j + 1],
                                                          None, ALU.add),
                 r=[bPB[bb_], bconsts], w=[bQT])
            P.op("vector", lambda e, j=j, cb_=cb_: e.tensor_scalar(QAB[:], PB[cb_][:],
                                                          BBC[:, KI + j:KI + j + 1], None, ALU.add),
                 r=[bPB[cb_], bconsts], w=[bQAB])
            a_b = AD[:, j:j + 1].broadcast_to([128, T])
            init = 0.0 if (c_pos == 0) else SCAR[:, j:j + 1]
            extra_r = [bconsts] if c_pos == 0 else [bconsts, bSCAR]
            P.op("vector", lambda e, a_b=a_b, init=init: e.tensor_tensor_scan(
                QSG[:], a_b, QT[:], init, ALU.mult, ALU.add),
                 r=[bQT] + extra_r, w=[bQSG])
            P.op("vector", lambda e, j=j: e.tensor_copy(SCAR[:, j:j + 1], QSG[:, T - 1:T]),
                 r=[bQSG], w=[bSCAR])
            P.op("vector", lambda e: e.tensor_mul(QT[:], QSG[:], QAB[:]), r=[bQSG, bQAB], w=[bQT])
            P.op("vector", lambda e, j=j: e.tensor_tensor(QT[:], QT[:], SIGG[:, j * T:(j + 1) * T],
                                                          ALU.mult),
                 r=[bSIGG], w=[bQT])
            jq = j % SECT_Y
            P.op("vector", lambda e, jq=jq: e.tensor_copy(AQ1[:, jq * T:(jq + 1) * T], QT[:]),
                 r=[bQT], w=[bAQ1])
            P.op("gpsimd", lambda e, jq=jq: e.tensor_tensor(AQ2[:, jq * T:(jq + 1) * T], QT[:],
                                                            AQ1[:, jq * T:(jq + 1) * T].bitcast(F32),
                                                            ALU.subtract),
                 r=[bQT, bAQ1], w=[bAQ2])
            if jq == SECT_Y - 1:
                sec = j // SECT_Y
                for mo in range(KD):
                    sl = next_slot()
                    po1 = POOL[sl][:, 0:SECT_Y * 128]
                    po2 = POOL[sl][:, SECT_Y * 128:2 * SECT_Y * 128]
                    c0 = sec * SECT_Y * 128
                    P.dma("qw2", lambda e, mo=mo, c0=c0, d=po1: e.dma_start(
                        out=d, in_=SO[0][mo, :, c0:c0 + SECT_Y * 128]), r=[bSO[0]], w=[bPOOL[sl]])
                    P.dma("qw2", lambda e, mo=mo, c0=c0, d=po2: e.dma_start(
                        out=d, in_=SO[1][mo, :, c0:c0 + SECT_Y * 128]), r=[bSO[1]], w=[bPOOL[sl]])
                    pbi = 0 + (mo % 2)
                    fns = []
                    for jj in range(SECT_Y):
                        st1 = po1[:, jj * 128:(jj + 1) * 128]
                        st2 = po2[:, jj * 128:(jj + 1) * 128]
                        m1 = AQ1[:, jj * T:(jj + 1) * T]
                        m2 = AQ2[:, jj * T:(jj + 1) * T]
                        fns.append(lambda e, st1=st1, m1=m1, jj=jj, pbi=pbi: e.matmul(
                            PB[pbi][:], st1, m1, start=(jj == 0), stop=False))
                        fns.append(lambda e, st1=st1, m2=m2, pbi=pbi: e.matmul(
                            PB[pbi][:], st1, m2, start=False, stop=False))
                        fns.append(lambda e, st2=st2, m1=m1, jj=jj, pbi=pbi: e.matmul(
                            PB[pbi][:], st2, m1, start=False, stop=(jj == SECT_Y - 1)))
                    P.tgroup(fns, r=[bAQ1, bAQ2, bPOOL[sl]], w=[bPB[pbi]])
                    xc = X[:, mo * T:(mo + 1) * T]
                    if sec == NSEC_Y - 1:
                        P.op("vector", lambda e, mo=mo, pbi=pbi: e.tensor_scalar(
                            QC[:], PB[pbi][:], BOUT[:, mo:mo + 1], None, ALU.add),
                             r=[bPB[pbi], bconsts], w=[bQC])
                        P.op("vector", lambda e, xc=xc: e.tensor_add(xc, xc, QC[:]), r=[bQC], w=[bXc[mo]])
                    else:
                        P.op("vector", lambda e, xc=xc, pbi=pbi: e.scalar_tensor_tensor(
                            xc, PB[pbi][:], 1.0, xc, ALU.bypass, ALU.add), r=[bPB[pbi]], w=[bXc[mo]])

    out_tags = []
    for t in range(NT):
        s, c_pos = t // NT_SEQ, t % NT_SEQ
        for ch in range(KD):
            P.dma("qa", lambda e, s=s, c_pos=c_pos, ch=ch: e.dma_start(
                out=X[:, ch * T:(ch + 1) * T],
                in_=xin[s, ch * 128:(ch + 1) * 128, c_pos * T:(c_pos + 1) * T]),
                r=[bXc[ch]], w=[bXc[ch]])
        ffn_sublayer()
        if t == 0:
            prep_mm_weight(wb, SB[0], SB[1], bSBc[0], bSBc[1], KI, DBC, deferred=True)
            prep_mm_weight(wo, SO[0], SO[1], bSO[0], bSO[1], KI, D, deferred=True)
        mamba_sublayer(c_pos)
        ffn_sublayer()
        for ch in range(KD):
            tg = P.dma_s(lambda e, s=s, c_pos=c_pos, ch=ch: e.dma_start(
                out=yout[s, ch * 128:(ch + 1) * 128, c_pos * T:(c_pos + 1) * T],
                in_=X[:, ch * T:(ch + 1) * T]),
                r=[bXc[ch]], w=[B("yout")])
            out_tags.append(tg)

    P.sync_wait(out_tags)

    import contextlib
    with contextlib.ExitStack() as stack:
        for k in P.semkeys:
            P.sems[k] = stack.enter_context(nc.semaphore(k))
        with nc.Block() as block:
            @block.sync
            def _(eng):
                for f in P.ops["sync"]:
                    f(eng)

            @block.vector
            def _(eng):
                for f in P.ops["vector"]:
                    f(eng)

            @block.scalar
            def _(eng):
                for f in P.ops["scalar"]:
                    f(eng)

            @block.tensor
            def _(eng):
                for f in P.ops["tensor"]:
                    f(eng)

            @block.gpsimd
            def _(eng):
                for f in P.ops["gpsimd"]:
                    f(eng)
    return nc


_NC = None


def kernel(L0_attn_x, L0_attn_W_in, L0_attn_b_in, L0_attn_conv_w, L0_attn_conv_b,
           L0_attn_W_bc, L0_attn_b_bc, L0_attn_log_decay, L0_attn_W_out, L0_attn_b_out,
           L0_ffn_w_gate, L0_ffn_w_up, L0_ffn_w_down):
    global _NC
    if _NC is None:
        _NC = _build_nc()
    nc = _NC

    f32 = np.float32
    x = np.asarray(L0_attn_x, f32)
    com = {
        "wg": np.ascontiguousarray(np.asarray(L0_ffn_w_gate, f32)),
        "wu": np.ascontiguousarray(np.asarray(L0_ffn_w_up, f32)),
        "wd": np.ascontiguousarray(np.asarray(L0_ffn_w_down, f32)),
        "wi": np.ascontiguousarray(np.asarray(L0_attn_W_in, f32)),
        "wb": np.ascontiguousarray(np.asarray(L0_attn_W_bc, f32)),
        "wo": np.ascontiguousarray(np.asarray(L0_attn_W_out, f32)),
        "bin": np.ascontiguousarray(np.asarray(L0_attn_b_in, f32).reshape(KB, 128).T),
        "bbc": np.ascontiguousarray(np.asarray(L0_attn_b_bc, f32).reshape(KB, 128).T),
        "bout": np.ascontiguousarray(np.asarray(L0_attn_b_out, f32).reshape(KD, 128).T),
        "cwd": np.ascontiguousarray(np.asarray(L0_attn_conv_w, f32).reshape(KI, 128, 4).transpose(1, 2, 0)),
        "cbd": np.ascontiguousarray(np.asarray(L0_attn_conv_b, f32).reshape(KI, 128).T),
        "ldd": np.ascontiguousarray(np.asarray(L0_attn_log_decay, f32).reshape(KI, 128).T),
    }
    in_maps = []
    for i in range(8):
        xc = np.ascontiguousarray(x[2 * i:2 * i + 2].transpose(0, 2, 1))
        m = dict(com)
        m["x"] = xc
        in_maps.append(m)
    res = run_bass_kernel_spmd(nc, in_maps, core_ids=list(range(8)))
    outs = []
    for i in range(8):
        yc = np.asarray(res.results[i]["y"])
        outs.append(yc.transpose(0, 2, 1))
    return np.ascontiguousarray(np.concatenate(outs, axis=0))


# version 23
# speedup vs baseline: 1.1216x; 1.1216x over previous
import numpy as np
import concourse.bass as bass
import concourse.mybir as mybir
from concourse.bass_utils import run_bass_kernel_spmd

F32 = mybir.dt.float32
F32R = mybir.dt.float32r
BF16 = mybir.dt.bfloat16
ALU = mybir.AluOpType
ACT = mybir.ActivationFunctionType
AX = mybir.AxisListType.X

D, DI, DBC, DH = 1024, 2048, 4096, 2816
KD, KI, KB, KH = D // 128, DI // 128, DBC // 128, DH // 128
T = 512
NSEQ, SEQ = 2, 2048
NT_SEQ = SEQ // T
NT = NSEQ * NT_SEQ
SECT_H = 11
NSEC_H = KH // SECT_H
SECT_Y = 8
NSEC_Y = KI // SECT_Y


_BIG = float(2.0 ** 40)
_MIDS = (0.25, 0.75, 1.25, 1.75, 2.5, 3.5, 5.0)
_SIGB = tuple(-float(2.0 ** 40) * float(np.nextafter(np.float32(m), np.float32(np.inf)))
              for m in _MIDS)

ENGS = ("sync", "vector", "scalar", "gpsimd", "tensor")
SELF = {"vector": "vS", "scalar": "sS", "gpsimd": "gS"}


class Buf:
    __slots__ = ("w", "rs")

    def __init__(self):
        self.w = {}
        self.rs = {}

    @staticmethod
    def _merge(d, tag):
        if tag is None:
            return
        k, v = tag
        if d.get(k, 0) < v:
            d[k] = v


class Plan:
    def __init__(self, nc):
        self.nc = nc
        self.ops = {e: [] for e in ENGS}
        self.cnt = {"vS": 0, "sS": 0, "gS": 0, "tS": 0}
        self.semkeys = ["vS", "sS", "gS", "tS"]
        for i in range(6):
            self.cnt[f"q{i}"] = 0
            self.semkeys.append(f"q{i}")
        for i in range(2):
            self.cnt[f"qs{i}"] = 0
            self.semkeys.append(f"qs{i}")
        self.dma_s_i = 0
        self.seen = {e: {} for e in ENGS}
        self.sems = {}
        self.dma_i = 0

    def _waits(self, eng, bufs_r, bufs_w, extra=()):
        tags = {}
        for b in bufs_r:
            for k, v in b.w.items():
                if tags.get(k, 0) < v:
                    tags[k] = v
        for b in bufs_w:
            for d in (b.w, b.rs):
                for k, v in d.items():
                    if tags.get(k, 0) < v:
                        tags[k] = v
        for k, v in extra:
            if tags.get(k, 0) < v:
                tags[k] = v
        out = []
        seen = self.seen[eng]
        skip = SELF.get(eng)
        for k, v in tags.items():
            if k == skip:
                continue
            if seen.get(k, 0) >= v:
                continue
            seen[k] = v
            out.append((k, v))
        return out

    def _post(self, tag, bufs_r, bufs_w):
        for b in bufs_r:
            Buf._merge(b.rs, tag)
        for b in bufs_w:
            b.w = {}
            b.rs = {}
            Buf._merge(b.w, tag)

    def op(self, eng, fn, r=(), w=(), extra=()):
        sk = SELF[eng]
        waits = self._waits(eng, r, w, extra)
        n = self.cnt[sk]
        self.cnt[sk] = n + 1

        def emit(e, fn=fn, waits=waits, n=n, sk=sk):
            for k, v in waits:
                e.wait_ge(self.sems[k], v)
            ins = fn(e)
            if n > 0:
                ins._wait_ge(self.sems[sk], n)
            ins.then_inc(self.sems[sk], 1)

        self.ops[eng].append(emit)
        tag = (sk, n + 1)
        self._post(tag, r, w)
        return tag

    def tgroup(self, fns, r=(), w=()):
        waits = self._waits("tensor", r, w)
        n = self.cnt["tS"] + 1
        self.cnt["tS"] = n

        def emit(e, fns=fns, waits=waits):
            for k, v in waits:
                e.wait_ge(self.sems[k], v)
            ins = None
            for fn in fns:
                ins = fn(e)
            ins.then_inc(self.sems["tS"], 1)

        self.ops["tensor"].append(emit)
        tag = ("tS", n)
        self._post(tag, r, w)
        return tag

    def dma_s(self, fn, r=(), w=()):
        q = f"qs{self.dma_s_i % 2}"
        self.dma_s_i += 1
        waits = self._waits("scalar", r, w)
        prev = self.cnt[q]
        self.cnt[q] = prev + 16

        def emit(e, fn=fn, waits=waits, q=q, prev=prev):
            for k, v in waits:
                e.wait_ge(self.sems[k], v)
            ins = fn(e)
            if prev > 0:
                ins._wait_ge(self.sems[q], prev)
            ins.then_inc(self.sems[q], 16)

        self.ops["scalar"].append(emit)
        tag = (q, prev + 16)
        self._post(tag, r, w)
        return tag

    def dma(self, _q_ignored, fn, r=(), w=()):
        q = f"q{self.dma_i % 6}"
        self.dma_i += 1
        waits = self._waits("sync", r, w)
        prev = self.cnt[q]
        self.cnt[q] = prev + 16

        def emit(e, fn=fn, waits=waits, q=q, prev=prev):
            for k, v in waits:
                e.wait_ge(self.sems[k], v)
            ins = fn(e)
            if prev > 0:
                ins._wait_ge(self.sems[q], prev)
            ins.then_inc(self.sems[q], 16)

        self.ops["sync"].append(emit)
        tag = (q, prev + 16)
        self._post(tag, r, w)
        return tag

    @staticmethod
    def merge_bufs(dst, srcs):
        for s in srcs:
            for k, v in s.w.items():
                if dst.w.get(k, 0) < v:
                    dst.w[k] = v
            for k, v in s.rs.items():
                if dst.rs.get(k, 0) < v:
                    dst.rs[k] = v

    def sync_wait(self, tags):
        waits = self._waits("sync", (), (), extra=tags)

        def emit(e, waits=waits):
            for k, v in waits:
                e.wait_ge(self.sems[k], v)

        self.ops["sync"].append(emit)


def _build_nc():
    nc = bass.Bass("TRN2", target_bir_lowering=False)
    P = Plan(nc)

    xin = nc.dram_tensor("x", [NSEQ, D, SEQ], F32, kind="ExternalInput").ap()
    yout = nc.dram_tensor("y", [NSEQ, D, SEQ], F32, kind="ExternalOutput").ap()
    wg = nc.dram_tensor("wg", [DH, D], F32, kind="ExternalInput").ap()
    wu = nc.dram_tensor("wu", [DH, D], F32, kind="ExternalInput").ap()
    wd = nc.dram_tensor("wd", [D, DH], F32, kind="ExternalInput").ap()
    wi = nc.dram_tensor("wi", [D, DBC], F32, kind="ExternalInput").ap()
    wb = nc.dram_tensor("wb", [DI, DBC], F32, kind="ExternalInput").ap()
    wo = nc.dram_tensor("wo", [DI, D], F32, kind="ExternalInput").ap()
    bin_ = nc.dram_tensor("bin", [128, KB], F32, kind="ExternalInput").ap()
    bbc = nc.dram_tensor("bbc", [128, KB], F32, kind="ExternalInput").ap()
    bout = nc.dram_tensor("bout", [128, KD], F32, kind="ExternalInput").ap()
    cwd = nc.dram_tensor("cwd", [128, 4, KI], F32, kind="ExternalInput").ap()
    cbd = nc.dram_tensor("cbd", [128, KI], F32, kind="ExternalInput").ap()
    ldd = nc.dram_tensor("ldd", [128, KI], F32, kind="ExternalInput").ap()
    ident_d = nc.inline_tensor(np.eye(128, dtype=np.float32), name="ident").ap()

    SG = [nc.dram_tensor(f"SG{i}", [KH, 128, KD * 128], F32R, kind="Internal").ap() for i in range(2)]
    SU = [nc.dram_tensor(f"SU{i}", [KH, 128, KD * 128], F32R, kind="Internal").ap() for i in range(2)]
    SD = [nc.dram_tensor(f"SD{i}", [KD, 128, KH * 128], F32R, kind="Internal").ap() for i in range(2)]
    SI = [nc.dram_tensor(f"SI{i}", [KB, 128, KD * 128], F32R, kind="Internal").ap() for i in range(2)]
    SB = [nc.dram_tensor(f"SB{i}", [KB, 128, KI * 128], F32R, kind="Internal").ap() for i in range(2)]
    SO = [nc.dram_tensor(f"SO{i}", [KD, 128, KI * 128], F32R, kind="Internal").ap() for i in range(2)]

    A = nc.alloc_sbuf_tensor
    X = A("X", [128, KD * T], F32)
    AQ1 = A("AQ1", [128, KD * T], F32R)
    AQ2 = A("AQ2", [128, KD * T], F32R)
    BIG1 = A("BIG1", [128, KI * T], F32R)
    BIG2 = A("BIG2", [128, KI * T], F32R)
    SIGG = A("SIGG", [128, KI * T], BF16)
    CH1 = A("CH1", [128, T], F32)
    CH2 = A("CH2", [128, T], F32)
    QT = A("QT", [128, T], F32)
    QAB = A("QAB", [128, T], F32)
    QSG = A("QSG", [128, T], F32)
    QC = A("QC", [128, T], F32)
    QD = A("QD", [128, T], F32)
    UW = A("UW", [128, T + 4], F32)
    QBM = A("QBM", [128, 176], F32)
    QSC = A("QSC", [128, 176], F32)
    QIV = A("QIV", [128, 176], F32)
    SMU = A("SMU", [1, T], F32)
    SVR = A("SVR", [1, T], F32)
    ST1 = A("ST1", [1, T], F32)
    SY0 = A("SY0", [1, T], F32)
    SIV = A("SIV", [1, T], F32)
    MUr = A("MUr", [1, T], F32R)
    IVr = A("IVr", [1, T], F32R)
    XRr = A("XRr", [128, T], F32R)
    XRr2 = A("XRr2", [128, T], F32R)
    SQF = A("SQF", [128, T], F32)
    SQR = A("SQR", [128, T], F32R)
    SQR2 = A("SQR2", [128, T], F32R)
    UH = A("UH", [128, KI * 3], F32)
    SCAR = A("SCAR", [128, KI], F32)
    ID = A("ID", [128, 128], F32)
    ONESC = A("ONESC", [128, 1], F32R)
    ONESCf = A("ONESCf", [128, 1], F32)
    ONESR = A("ONESR", [1, 128], F32R)
    ONESRf = A("ONESRf", [1, 128], F32)
    BIN = A("BIN", [128, KB], F32)
    BBC = A("BBC", [128, KB], F32)
    BOUT = A("BOUT", [128, KD], F32)
    CW = A("CW", [128, 4 * KI], F32)
    CB = A("CB", [128, KI], F32)
    LDT = A("LDT", [128, KI], F32)
    AD = A("AD", [128, KI], F32)
    SGB = A("SGB", [128, 8], F32)
    T16A = A("T16A", [128, KI], F32)
    T16B = A("T16B", [128, KI], F32)
    POOL = [A(f"POOL{i}", [128, KI * 128], F32R) for i in range(4)]

    PB = [nc.alloc_psum_tensor(f"pb{i}", [128, T], F32) for i in range(8)]

    bufs = {}

    def B(name):
        if name not in bufs:
            bufs[name] = Buf()
        return bufs[name]

    bXc = [B(f"Xc{i}") for i in range(KD)]
    bXB = B("XB")
    bAQ1, bAQ2, bBIG1, bBIG2, bSIGG = B("AQ1"), B("AQ2"), B("BIG1"), B("BIG2"), B("SIGG")
    bPT = [B("PTa"), B("PTb")]
    bW1p = [B("W1a"), B("W1b")]
    bW2p = [B("W2a"), B("W2b")]
    bCH1, bQT, bQAB, bQSG, bQC, bQD = B("CH1"), B("QT"), B("QAB"), B("QSG"), B("QC"), B("QD")
    bCH2 = B("CH2")
    bUW, bQBM, bQSC, bQIV, bSM = B("UW"), B("QBM"), B("QSC"), B("QIV"), B("SM")
    bMUr, bIVr, bXRr, bSQF, bSQR = B("MUr"), B("IVr"), B("XRr"), B("SQF"), B("SQR")
    bXRr2, bSQR2 = B("XRr2"), B("SQR2")
    bUH, bSCAR = B("UH"), B("SCAR")
    bPOOL = [B(f"POOL{i}") for i in range(4)]
    bPB = [B(f"pb{i}") for i in range(8)]
    bSG = [B("SG0"), B("SG1")]
    bSU = [B("SU0"), B("SU1")]
    bSD = [B("SD0"), B("SD1")]
    bSI = [B("SI0"), B("SI1")]
    bSBc = [B("SBc0"), B("SBc1")]
    bSO = [B("SO0"), B("SO1")]
    bconsts = B("consts")

    P.dma("qa", lambda e: e.dma_start(out=ID[:], in_=ident_d), w=[bconsts])
    for src, dst in ((bin_, BIN), (bbc, BBC), (bout, BOUT), (cbd, CB), (ldd, LDT)):
        P.dma("qa", lambda e, s=src, d=dst: e.dma_start(out=d[:], in_=s), w=[bconsts])
    P.dma("qa", lambda e: e.dma_start(
        out=CW[:].rearrange("p (a j) -> p a j", a=4), in_=cwd), w=[bconsts])
    for i in range(7):
        P.op("vector", lambda e, i=i: e.memset(SGB[:, i:i + 1], _SIGB[i]), w=[bconsts])
    P.op("vector", lambda e: e.memset(ONESCf[:], 1.0), w=[bconsts])
    P.op("vector", lambda e: e.memset(ONESRf[:], 1.0), w=[bconsts])
    P.op("vector", lambda e: e.tensor_copy(ONESC[:], ONESCf[:]), r=[bconsts], w=[bconsts])
    P.op("vector", lambda e: e.tensor_copy(ONESR[:], ONESRf[:]), r=[bconsts], w=[bconsts])
    P.op("scalar", lambda e: e.activation(T16A[:], LDT[:], ACT.Abs, scale=-1.0), r=[bconsts], w=[B("T16A")])
    P.op("scalar", lambda e: e.activation(T16A[:], T16A[:], ACT.Exp), r=[B("T16A")], w=[B("T16A")])
    P.op("vector", lambda e: e.tensor_single_scalar(T16A[:], T16A[:], 1.0, ALU.add), r=[B("T16A")], w=[B("T16A")])
    P.op("scalar", lambda e: e.activation(T16A[:], T16A[:], ACT.Ln), r=[B("T16A")], w=[B("T16A")])
    P.op("vector", lambda e: e.tensor_single_scalar(T16B[:], LDT[:], 0.0, ALU.max), r=[bconsts], w=[B("T16B")])
    P.op("vector", lambda e: e.tensor_add(T16A[:], T16A[:], T16B[:]), r=[B("T16A"), B("T16B")], w=[B("T16A")])
    P.op("scalar", lambda e: e.activation(AD[:], T16A[:], ACT.Exp, scale=-1.0), r=[B("T16A")], w=[bconsts])

    class Scr:
        def __init__(self, qt, qab, qsg, qc, qd, qbm, qsc, qiv,
                     bqt, bqab, bqsg, bqc, bqd, bqbm, bqsc, bqiv, qe=None, bqe=None):
            self.QT, self.QAB, self.QSG, self.QC, self.QD = qt, qab, qsg, qc, qd
            self.QBM, self.QSC, self.QIV = qbm, qsc, qiv
            self.bQT, self.bQAB, self.bQSG, self.bQC, self.bQD = bqt, bqab, bqsg, bqc, bqd
            self.bQBM, self.bQSC, self.bQIV = bqbm, bqsc, bqiv
            self.QE, self.bQE = qe, bqe

    SCR_A = Scr(QT[:], QAB[:], QSG[:], QC[:], QD[:],
                QBM[:], QSC[:], QIV[:],
                bQT, bQAB, bQSG, bQC, bQD, bQBM, bQSC, bQIV,
                qe=SQF[:], bqe=bSQF)
    SCR_B = Scr(X[:, 0:512], X[:, 512:1024], X[:, 1024:1536], X[:, 1536:2048],
                X[:, 2048:2560], X[:, 2560:2736], X[:, 2736:2912], X[:, 2912:3088],
                bXB, bXB, bXB, bXB, bXB, bXB, bXB, bXB,
                qe=X[:, 3088:3600], bqe=bXB)

    def stair_core(S, w):
        qc, qd, ab = S.QC[:, 0:w], S.QD[:, 0:w], S.QAB[:, 0:w]
        P.op("vector", lambda e: e.tensor_single_scalar(qc, ab, 0.25, ALU.is_gt),
             r=[S.bQAB], w=[S.bQC])
        for th in (0.75, 1.25, 1.75):
            P.op("vector", lambda e, th=th: e.scalar_tensor_tensor(qc, ab, th, qc, ALU.is_gt, ALU.add),
                 r=[S.bQAB], w=[S.bQC])
        P.op("vector", lambda e: e.tensor_single_scalar(qd, ab, 2.5, ALU.is_gt),
             r=[S.bQAB], w=[S.bQD])
        P.op("vector", lambda e: e.scalar_tensor_tensor(qd, ab, 3.5, qd, ALU.is_gt, ALU.add),
             r=[S.bQAB], w=[S.bQD])
        P.op("vector", lambda e: e.scalar_tensor_tensor(qc, qc, 0.5, qd, ALU.mult, ALU.add),
             r=[S.bQD], w=[S.bQC])
        P.op("vector", lambda e: e.tensor_scalar(qd, ab, 5.0, 2.0, ALU.is_gt, ALU.mult),
             r=[S.bQAB], w=[S.bQD])
        P.op("vector", lambda e: e.tensor_add(qc, qc, qd), r=[S.bQD], w=[S.bQC])

    def quant_core(S, gdim, nblk, w, src_ap=None, src_b=None):
        if src_ap is None:
            src_ap, src_b = S.QT[:, 0:w], S.bQT
        s4 = src_ap.rearrange("p (g b k) -> p g b k", g=gdim, k=16)
        v4 = S.QT[:, 0:w].rearrange("p (g b k) -> p g b k", g=gdim, k=16)
        bm = S.QBM[:, 0:gdim * nblk].rearrange("p (g b) -> p g b", g=gdim)
        sc = S.QSC[:, 0:gdim * nblk]
        iv = S.QIV[:, 0:gdim * nblk]
        P.op("vector", lambda e: e.tensor_reduce(bm, s4, AX, ALU.max, apply_absolute_value=True),
             r=[src_b], w=[S.bQBM])
        P.op("vector", lambda e: e.tensor_scalar(sc, S.QBM[:, 0:gdim * nblk], 1.0 / 6.0, 1e-12,
                                                 ALU.mult, ALU.max),
             r=[S.bQBM], w=[S.bQSC])
        P.op("vector", lambda e: e.reciprocal(iv, sc), r=[S.bQSC], w=[S.bQIV])
        ivb = iv.rearrange("p (g b) -> p g b", g=gdim).unsqueeze(3).broadcast_to([128, gdim, nblk, 16])
        scb = sc.rearrange("p (g b) -> p g b", g=gdim).unsqueeze(3).broadcast_to([128, gdim, nblk, 16])
        P.op("vector", lambda e: e.tensor_tensor(v4, s4, ivb, ALU.mult), r=[S.bQIV, src_b], w=[S.bQT])
        P.op("scalar", lambda e: e.activation(S.QAB[:, 0:w], S.QT[:, 0:w], ACT.Abs),
             r=[S.bQT], w=[S.bQAB])
        P.op("scalar", lambda e: e.activation(S.QSG[:, 0:w], S.QT[:, 0:w], ACT.Sign),
             r=[S.bQT], w=[S.bQSG])
        qc, qd, qe, ab = S.QC[:, 0:w], S.QD[:, 0:w], S.QE[:, 0:w], S.QAB[:, 0:w]
        P.op("vector", lambda e: e.tensor_single_scalar(qc, ab, 0.25, ALU.is_gt),
             r=[S.bQAB], w=[S.bQC])
        for th in (0.75, 1.25, 1.75):
            P.op("vector", lambda e, th=th: e.scalar_tensor_tensor(qc, ab, th, qc, ALU.is_gt, ALU.add),
                 r=[S.bQAB], w=[S.bQC])
        P.op("vector", lambda e: e.tensor_single_scalar(qd, ab, 2.5, ALU.is_gt),
             r=[S.bQAB], w=[S.bQD])
        P.op("vector", lambda e: e.scalar_tensor_tensor(qd, ab, 3.5, qd, ALU.is_gt, ALU.add),
             r=[S.bQAB], w=[S.bQD])
        P.op("vector", lambda e: e.tensor_scalar(qe, ab, 5.0, 2.0, ALU.is_gt, ALU.mult),
             r=[S.bQAB], w=[S.bQE])
        P.op("vector", lambda e: e.scalar_tensor_tensor(qc, qc, 0.5, qd, ALU.mult, ALU.add),
             r=[S.bQD], w=[S.bQC])
        P.op("vector", lambda e: e.tensor_add(qc, qc, qe), r=[S.bQE], w=[S.bQC])
        P.op("vector", lambda e: e.tensor_mul(qc, qc, S.QSG[:, 0:w]), r=[S.bQSG], w=[S.bQC])
        qc4 = S.QC[:, 0:w].rearrange("p (g b k) -> p g b k", g=gdim, k=16)
        P.op("vector", lambda e: e.tensor_tensor(v4, qc4, scb, ALU.mult), r=[S.bQC, S.bQSC], w=[S.bQT])

    def quant_front(src_sb, src_buf, pbA):
        bA = bPB[pbA]
        P.tgroup([(lambda e, g=g, pbA=pbA, src_sb=src_sb: e.transpose(
            PB[pbA][:, g * 128:(g + 1) * 128], src_sb[:, g * 128:(g + 1) * 128], ID[:]))
                  for g in range(4)], r=[src_buf, bconsts], w=[bA])

    def quant_back(q1_ap, q2_ap, dst1_buf, dst2_buf, nblk, pbB, pbB_src=0):
        bB = bPB[pbB]
        quant_core(SCR_A, 4, nblk, T, src_ap=PB[pbB_src][:], src_b=bPB[pbB_src])
        P.tgroup([(lambda e, g=g, pbB=pbB: e.transpose(
            PB[pbB][:, g * 128:(g + 1) * 128], QT[:, g * 128:(g + 1) * 128], ID[:]))
                  for g in range(4)], r=[bQT, bconsts], w=[bB])
        P.op("vector", lambda e, pbB=pbB: e.tensor_copy(q1_ap, PB[pbB][:]), r=[bB], w=[dst1_buf])
        P.op("vector", lambda e, pbB=pbB: e.tensor_tensor(q2_ap, PB[pbB][:], q1_ap.bitcast(F32),
                                                          ALU.subtract),
             r=[bB, dst1_buf], w=[dst2_buf])

    def quant_tile(src_sb, src_buf, q1_ap, q2_ap, dst1_buf, dst2_buf, nblk, pbA, pbB):
        quant_front(src_sb, src_buf, pbA)
        quant_back(q1_ap, q2_ap, dst1_buf, dst2_buf, nblk, pbB, pbB_src=pbA)

    def prep_ffn_weight(src, s1, s2, bs1, bs2, nrows_chunks, kfree):
        pts = [(AQ2[:, 0:512], AQ2[:, 512:1024]), (AQ2[:, 1024:1536], AQ2[:, 1536:2048])]
        pieces = [(j, c0) for j in range(nrows_chunks) for c0 in range(0, kfree, 512)]
        pend = None

        def flush(pend):
            (j, c0, wdt, pt1, pt2, bpt, pbi) = pend
            P.op("vector", lambda e, wdt=wdt, pt1=pt1, pbi=pbi: e.tensor_copy(
                pt1[:, 0:wdt], PB[pbi][:, 0:wdt]), r=[bPB[pbi]], w=[bpt])
            P.op("vector", lambda e, wdt=wdt, pt1=pt1, pt2=pt2, pbi=pbi: e.tensor_tensor(
                pt2[:, 0:wdt], PB[pbi][:, 0:wdt], pt1[:, 0:wdt].bitcast(F32), ALU.subtract),
                r=[bPB[pbi]], w=[bpt])
            P.dma_s(lambda e, j=j, c0=c0, wdt=wdt, pt1=pt1: e.dma_start(
                out=s1[j, :, c0:c0 + wdt], in_=pt1[:, 0:wdt]), r=[bpt], w=[bs1])
            P.dma_s(lambda e, j=j, c0=c0, wdt=wdt, pt2=pt2: e.dma_start(
                out=s2[j, :, c0:c0 + wdt], in_=pt2[:, 0:wdt]), r=[bpt], w=[bs2])

        for pi, (j, c0) in enumerate(pieces):
            S = SCR_A if pi % 2 == 0 else SCR_B
            pt1, pt2 = pts[pi % 2]
            bpt = bPT[pi % 2]
            pbi = pi % 2
            cw_ = min(512, kfree - c0)
            nb = cw_ // 16
            nk = cw_ // 128
            P.dma("qa", lambda e, j=j, c0=c0, cw_=cw_, S=S: e.dma_start(
                out=S.QT[:, 0:cw_], in_=src[j * 128:(j + 1) * 128, c0:c0 + cw_]),
                r=[S.bQT], w=[S.bQT])
            quant_core(S, 1, nb, cw_)
            P.tgroup([(lambda e, kk=kk, S=S, pbi=pbi: e.transpose(
                PB[pbi][:, kk * 128:(kk + 1) * 128], S.QT[:, kk * 128:(kk + 1) * 128], ID[:]))
                      for kk in range(nk)], r=[S.bQT, bconsts], w=[bPB[pbi]])
            if pend is not None:
                flush(pend)
            pend = (j, c0, nk * 128, pt1, pt2, bpt, pbi)
        flush(pend)

    prep_ffn_weight(wg, SG[0], SG[1], bSG[0], bSG[1], KH, D)
    prep_ffn_weight(wu, SU[0], SU[1], bSU[0], bSU[1], KH, D)
    prep_ffn_weight(wd, SD[0], SD[1], bSD[0], bSD[1], KD, DH)

    def prep_mm_weight(src, s1, s2, bs1, bs2, kchunks, m, deferred=False):
        mb = m // 128
        pieces = [(k, b0) for k in range(kchunks) for b0 in range(0, mb, 4)]
        for pi, (k, b0) in enumerate(pieces):
            if deferred:
                S = SCR_A
                w1 = BIG1[:, 11 * T:11 * T + 512]
                w2 = BIG2[:, 11 * T:11 * T + 512]
                bw1, bw2 = bBIG1, bBIG2
            else:
                S = SCR_A if pi % 2 == 0 else SCR_B
                w1 = AQ1[:, 0:512] if pi % 2 == 0 else AQ1[:, 512:1024]
                w2 = AQ2[:, 0:512] if pi % 2 == 0 else AQ2[:, 512:1024]
                bw1, bw2 = bW1p[pi % 2], bW2p[pi % 2]
            bw = min(4, mb - b0)
            cw_ = bw * 128
            P.dma("qa", lambda e, k=k, b0=b0, cw_=cw_, S=S: e.dma_start(
                out=S.QT[:, 0:cw_], in_=src[k * 128:(k + 1) * 128, b0 * 128:b0 * 128 + cw_]),
                r=[S.bQT], w=[S.bQT])
            P.op("vector", lambda e, cw_=cw_, S=S, w1=w1: e.tensor_copy(w1[:, 0:cw_], S.QT[:, 0:cw_]),
                 r=[S.bQT], w=[bw1])
            P.op("gpsimd", lambda e, cw_=cw_, S=S, w1=w1, w2=w2: e.tensor_tensor(
                w2[:, 0:cw_], S.QT[:, 0:cw_], w1[:, 0:cw_].bitcast(F32), ALU.subtract),
                r=[S.bQT, bw1], w=[bw2])
            for ws, bws, sd, bd in ((w1, bw1, s1, bs1), (w2, bw2, s2, bs2)):
                P.dma_s(lambda e, ws=ws, sd=sd, k=k, b0=b0, bw=bw, cw_=cw_: e.dma_start(
                    out=sd[b0:b0 + bw, :, k * 128:(k + 1) * 128].transpose([1, 0, 2]),
                    in_=ws[:, 0:cw_].rearrange("p (b q) -> p b q", b=bw)),
                    r=[bws], w=[bd])

    Plan.merge_bufs(bW2p[0], bPT)
    Plan.merge_bufs(bW2p[1], bPT)
    Plan.merge_bufs(bW1p[0], bPT)
    Plan.merge_bufs(bW1p[1], bPT)
    prep_mm_weight(wi, SI[0], SI[1], bSI[0], bSI[1], KD, DBC)


    Plan.merge_bufs(bAQ1, bW1p)
    Plan.merge_bufs(bAQ2, bPT + bW2p)
    for _b in bXc:
        Plan.merge_bufs(_b, [bXB])

    def layernorm_prep():
        for ch in range(KD):
            xc = X[:, ch * T:(ch + 1) * T]
            xr, bxr = (XRr, bXRr) if ch % 2 == 0 else (XRr2, bXRr2)
            sq, bsq = (SQR, bSQR) if ch % 2 == 0 else (SQR2, bSQR2)
            P.op("vector", lambda e, xc=xc, xr=xr: e.tensor_copy(xr[:], xc), r=[bXc[ch]], w=[bxr])
            P.tgroup([(lambda e, ch=ch, xr=xr: e.matmul(PB[6][0:1, :], ONESC[:], xr[:],
                                                        start=(ch == 0), stop=(ch == KD - 1),
                                                        skip_group_check=True))],
                     r=[bxr, bconsts], w=[bPB[6]])
            P.op("vector", lambda e, xc=xc: e.tensor_tensor(SQF[:], xc, xc, ALU.mult), r=[bXc[ch]], w=[bSQF])
            P.op("gpsimd", lambda e, sq=sq: e.tensor_copy(sq[:], SQF[:]), r=[bSQF], w=[bsq])
            P.tgroup([(lambda e, ch=ch, sq=sq: e.matmul(PB[7][0:1, :], ONESC[:], sq[:],
                                                        start=(ch == 0), stop=(ch == KD - 1),
                                                        skip_group_check=True))],
                     r=[bsq, bconsts], w=[bPB[7]])
        mu, var, t1, y0, ivs = SMU[:], SVR[:], ST1[:], SY0[:], SIV[:]
        P.op("vector", lambda e: e.tensor_single_scalar(mu, PB[6][0:1, :], 1.0 / D, ALU.mult),
             r=[bPB[6]], w=[bSM])
        P.op("vector", lambda e: e.tensor_single_scalar(var, PB[7][0:1, :], 1.0 / D, ALU.mult),
             r=[bPB[7]], w=[bSM])
        P.op("vector", lambda e: e.tensor_mul(t1, mu, mu), r=[bSM], w=[bSM])
        P.op("vector", lambda e: e.tensor_sub(var, var, t1), r=[bSM], w=[bSM])
        P.op("vector", lambda e: e.tensor_single_scalar(var, var, 1e-5, ALU.add), r=[bSM], w=[bSM])
        P.op("vector", lambda e: e.reciprocal(t1, var), r=[bSM], w=[bSM])
        P.op("scalar", lambda e: e.activation(y0, t1, ACT.Sqrt), r=[bSM], w=[bSM])
        P.op("vector", lambda e: e.tensor_mul(ivs, y0, y0), r=[bSM], w=[bSM])
        P.op("vector", lambda e: e.tensor_mul(ivs, ivs, var), r=[bSM], w=[bSM])
        P.op("vector", lambda e: e.tensor_scalar(ivs, ivs, -0.5, 1.5, ALU.mult, ALU.add), r=[bSM], w=[bSM])
        P.op("vector", lambda e: e.tensor_mul(ivs, ivs, y0), r=[bSM], w=[bSM])
        P.op("vector", lambda e: e.tensor_copy(MUr[:], mu), r=[bSM], w=[bMUr])
        P.op("vector", lambda e: e.tensor_copy(IVr[:], ivs), r=[bSM], w=[bIVr])
        P.tgroup([(lambda e: e.matmul(PB[4][:], ONESR[:], MUr[:], start=True, stop=True))],
                 r=[bMUr, bconsts], w=[bPB[4]])
        P.tgroup([(lambda e: e.matmul(PB[5][:], ONESR[:], IVr[:], start=True, stop=True))],
                 r=[bIVr, bconsts], w=[bPB[5]])

    def ln_chunk(ch):
        xc = X[:, ch * T:(ch + 1) * T]
        P.op("vector", lambda e, xc=xc: e.scalar_tensor_tensor(CH1[:], xc, 1.0, PB[4][:],
                                                               ALU.bypass, ALU.subtract),
             r=[bXc[ch], bPB[4]], w=[bCH1])
        P.op("vector", lambda e: e.tensor_tensor(CH1[:], CH1[:], PB[5][:], ALU.mult),
             r=[bPB[5]], w=[bCH1])

    qwkey_f = "qw1"

    pool_i = [0]

    def next_slot():
        i = pool_i[0] % 4
        pool_i[0] += 1
        return i

    def ffn_sublayer():
        layernorm_prep()
        for ch in range(KD):
            ln_chunk(ch)
            quant_tile(CH1, bCH1, AQ1[:, ch * T:(ch + 1) * T], AQ2[:, ch * T:(ch + 1) * T],
                       bAQ1, bAQ2, KD, 0, 1)
        for s in range(NSEC_H):
            ms = list(range(s * SECT_H, (s + 1) * SECT_H))

            def fwdT(rec):
                quant_front(rec[1], rec[2], rec[3])

            def chain(rec):
                quant_core(SCR_A, 4, KD, T, src_ap=PB[rec[3]][:], src_b=bPB[rec[3]])

            def backT(rec):
                ci, pbB = rec[0], rec[4]
                bB = bPB[pbB]
                P.tgroup([(lambda e, g=g, pbB=pbB: e.transpose(
                    PB[pbB][:, g * 128:(g + 1) * 128], QT[:, g * 128:(g + 1) * 128], ID[:]))
                          for g in range(4)], r=[bQT, bconsts], w=[bB])
                q1_ap = BIG1[:, ci * T:(ci + 1) * T]
                q2_ap = BIG2[:, ci * T:(ci + 1) * T]
                P.op("vector", lambda e, pbB=pbB, q1_ap=q1_ap: e.tensor_copy(q1_ap, PB[pbB][:]),
                     r=[bB], w=[bBIG1])
                P.op("vector", lambda e, pbB=pbB, q1_ap=q1_ap, q2_ap=q2_ap: e.tensor_tensor(
                    q2_ap, PB[pbB][:], q1_ap.bitcast(F32), ALU.subtract),
                     r=[bB, bBIG1], w=[bBIG2])

            p1, p2 = None, None
            for ii, mb in enumerate(ms):
                if p2 is not None:
                    backT(p2)
                if p1 is not None:
                    fwdT(p1)
                sl_a, sl_b = next_slot(), next_slot()
                pg1, pu1 = POOL[sl_a][:, 0:KD * 128], POOL[sl_a][:, KD * 128:2 * KD * 128]
                pg2, pu2 = POOL[sl_b][:, 0:KD * 128], POOL[sl_b][:, KD * 128:2 * KD * 128]
                P.dma(qwkey_f, lambda e, mb=mb, d=pg1: e.dma_start(out=d, in_=SG[0][mb]), r=[bSG[0]], w=[bPOOL[sl_a]])
                P.dma(qwkey_f, lambda e, mb=mb, d=pu1: e.dma_start(out=d, in_=SU[0][mb]), r=[bSU[0]], w=[bPOOL[sl_a]])
                P.dma(qwkey_f, lambda e, mb=mb, d=pg2: e.dma_start(out=d, in_=SG[1][mb]), r=[bSG[1]], w=[bPOOL[sl_b]])
                P.dma(qwkey_f, lambda e, mb=mb, d=pu2: e.dma_start(out=d, in_=SU[1][mb]), r=[bSU[1]], w=[bPOOL[sl_b]])
                gb, ub = 2 + 2 * (ii % 2), 3 + 2 * (ii % 2)
                for (pp1, pp2, pbi) in ((pg1, pg2, gb), (pu1, pu2, ub)):
                    fns = []
                    for k in range(KD):
                        st1 = pp1[:, k * 128:(k + 1) * 128]
                        st2 = pp2[:, k * 128:(k + 1) * 128]
                        m1 = AQ1[:, k * T:(k + 1) * T]
                        m2 = AQ2[:, k * T:(k + 1) * T]
                        fns.append(lambda e, st1=st1, m1=m1, k=k, pbi=pbi: e.matmul(
                            PB[pbi][:], st1, m1, start=(k == 0), stop=False))
                        fns.append(lambda e, st1=st1, m2=m2, pbi=pbi: e.matmul(
                            PB[pbi][:], st1, m2, start=False, stop=False))
                        fns.append(lambda e, st2=st2, m1=m1, k=k, pbi=pbi: e.matmul(
                            PB[pbi][:], st2, m1, start=False, stop=(k == KD - 1)))
                    P.tgroup(fns, r=[bAQ1, bAQ2, bPOOL[sl_a], bPOOL[sl_b]], w=[bPB[pbi]])
                if p1 is not None:
                    chain(p1)
                chs, chb = (CH1, bCH1) if ii % 2 == 0 else (CH2, bCH2)
                P.op("scalar", lambda e, chs=chs, gb=gb: e.activation(chs[:], PB[gb][:], ACT.Sigmoid),
                     r=[bPB[gb]], w=[chb])
                P.op("vector", lambda e, chs=chs, gb=gb: e.tensor_tensor(chs[:], chs[:], PB[gb][:],
                                                                         ALU.mult),
                     r=[bPB[gb]], w=[chb])
                P.op("vector", lambda e, chs=chs, ub=ub: e.tensor_tensor(chs[:], chs[:], PB[ub][:],
                                                                         ALU.mult),
                     r=[bPB[ub]], w=[chb])
                p2 = p1
                p1 = (mb - s * SECT_H, chs, chb,
                      0 if ii % 2 == 0 else 6, 1 if ii % 2 == 0 else 7)
            if p2 is not None:
                backT(p2)
            fwdT(p1)
            chain(p1)
            backT(p1)
            for mo in range(KD):
                sl = next_slot()
                sl2 = next_slot()
                pd1 = POOL[sl][:, 0:SECT_H * 128]
                pd2 = POOL[sl2][:, 0:SECT_H * 128]
                c0 = s * SECT_H * 128
                P.dma(qwkey_f, lambda e, mo=mo, c0=c0, d=pd1: e.dma_start(
                    out=d, in_=SD[0][mo, :, c0:c0 + SECT_H * 128]), r=[bSD[0]], w=[bPOOL[sl]])
                P.dma(qwkey_f, lambda e, mo=mo, c0=c0, d=pd2: e.dma_start(
                    out=d, in_=SD[1][mo, :, c0:c0 + SECT_H * 128]), r=[bSD[1]], w=[bPOOL[sl2]])
                pbi = 2 + (mo % 2)
                fns = []
                for ci in range(SECT_H):
                    st1 = pd1[:, ci * 128:(ci + 1) * 128]
                    st2 = pd2[:, ci * 128:(ci + 1) * 128]
                    m1 = BIG1[:, ci * T:(ci + 1) * T]
                    m2 = BIG2[:, ci * T:(ci + 1) * T]
                    fns.append(lambda e, st1=st1, m1=m1, ci=ci, pbi=pbi: e.matmul(
                        PB[pbi][:], st1, m1, start=(ci == 0), stop=False))
                    fns.append(lambda e, st1=st1, m2=m2, pbi=pbi: e.matmul(
                        PB[pbi][:], st1, m2, start=False, stop=False))
                    fns.append(lambda e, st2=st2, m1=m1, ci=ci, pbi=pbi: e.matmul(
                        PB[pbi][:], st2, m1, start=False, stop=(ci == SECT_H - 1)))
                P.tgroup(fns, r=[bBIG1, bBIG2, bPOOL[sl], bPOOL[sl2]], w=[bPB[pbi]])
                xc = X[:, mo * T:(mo + 1) * T]
                P.op("vector", lambda e, xc=xc, pbi=pbi: e.scalar_tensor_tensor(
                    xc, PB[pbi][:], 0.5, xc, ALU.mult, ALU.add), r=[bPB[pbi]], w=[bXc[mo]])

    def mamba_sublayer(c_pos):
        layernorm_prep()
        for ch in range(KD):
            ln_chunk(ch)
            P.op("vector", lambda e, ch=ch: e.tensor_copy(AQ1[:, ch * T:(ch + 1) * T], CH1[:]),
                 r=[bCH1], w=[bAQ1])
            P.op("gpsimd", lambda e, ch=ch: e.tensor_tensor(AQ2[:, ch * T:(ch + 1) * T], CH1[:],
                                                            AQ1[:, ch * T:(ch + 1) * T].bitcast(F32),
                                                            ALU.subtract),
                 r=[bCH1, bAQ1], w=[bAQ2])
        for j in range(KI):
            sl_a, sl_b = next_slot(), next_slot()
            piu1, piu2 = POOL[sl_a][:, 0:KD * 128], POOL[sl_a][:, KD * 128:2 * KD * 128]
            pig1, pig2 = POOL[sl_b][:, 0:KD * 128], POOL[sl_b][:, KD * 128:2 * KD * 128]
            P.dma("qw2", lambda e, j=j, d=piu1: e.dma_start(out=d, in_=SI[0][j]), r=[bSI[0]], w=[bPOOL[sl_a]])
            P.dma("qw2", lambda e, j=j, d=piu2: e.dma_start(out=d, in_=SI[1][j]), r=[bSI[1]], w=[bPOOL[sl_a]])
            P.dma("qw2", lambda e, j=j, d=pig1: e.dma_start(out=d, in_=SI[0][KI + j]), r=[bSI[0]], w=[bPOOL[sl_b]])
            P.dma("qw2", lambda e, j=j, d=pig2: e.dma_start(out=d, in_=SI[1][KI + j]), r=[bSI[1]], w=[bPOOL[sl_b]])
            ub_, gb_ = 2 + 2 * (j % 2), 3 + 2 * (j % 2)
            for (p1, p2, pbi) in ((piu1, piu2, ub_), (pig1, pig2, gb_)):
                fns = []
                for k in range(KD):
                    st1 = p1[:, k * 128:(k + 1) * 128]
                    st2 = p2[:, k * 128:(k + 1) * 128]
                    m1 = AQ1[:, k * T:(k + 1) * T]
                    m2 = AQ2[:, k * T:(k + 1) * T]
                    fns.append(lambda e, st1=st1, m1=m1, k=k, pbi=pbi: e.matmul(
                        PB[pbi][:], st1, m1, start=(k == 0), stop=False))
                    fns.append(lambda e, st1=st1, m2=m2, pbi=pbi: e.matmul(
                        PB[pbi][:], st1, m2, start=False, stop=False))
                    fns.append(lambda e, st2=st2, m1=m1, k=k, pbi=pbi: e.matmul(
                        PB[pbi][:], st2, m1, start=False, stop=(k == KD - 1)))
                P.tgroup(fns, r=[bAQ1, bAQ2, bPOOL[sl_a], bPOOL[sl_b]], w=[bPB[pbi]])
            if c_pos == 0:
                P.op("vector", lambda e: e.memset(UW[:, 0:3], 0.0), w=[bUW])
            else:
                P.op("vector", lambda e, j=j: e.tensor_copy(UW[:, 0:3], UH[:, j * 3:j * 3 + 3]),
                     r=[bUH], w=[bUW])
            P.op("vector", lambda e, j=j, ub_=ub_: e.tensor_scalar(UW[:, 3:3 + T], PB[ub_][:],
                                                          BIN[:, j:j + 1], None, ALU.add),
                 r=[bPB[ub_], bconsts], w=[bUW])
            P.op("vector", lambda e, j=j: e.tensor_copy(UH[:, j * 3:j * 3 + 3], UW[:, T:T + 3]),
                 r=[bUW], w=[bUH])
            P.op("vector", lambda e, j=j: e.tensor_scalar(QC[:], UW[:, 0:T], CW[:, 0 * KI + j:0 * KI + j + 1],
                                                          None, ALU.mult),
                 r=[bUW, bconsts], w=[bQC])
            P.op("vector", lambda e, j=j: e.scalar_tensor_tensor(QD[:], UW[:, 1:1 + T],
                                                                 CW[:, 1 * KI + j:1 * KI + j + 1],
                                                                 QC[:], ALU.mult, ALU.add),
                 r=[bUW, bQC, bconsts], w=[bQD])
            P.op("vector", lambda e, j=j: e.scalar_tensor_tensor(QC[:], UW[:, 2:2 + T],
                                                                 CW[:, 2 * KI + j:2 * KI + j + 1],
                                                                 QD[:], ALU.mult, ALU.add),
                 r=[bUW, bQD, bconsts], w=[bQC])
            P.op("vector", lambda e, j=j: e.scalar_tensor_tensor(QD[:], UW[:, 3:3 + T],
                                                                 CW[:, 3 * KI + j:3 * KI + j + 1],
                                                                 QC[:], ALU.mult, ALU.add),
                 r=[bUW, bQC, bconsts], w=[bQD])
            P.op("vector", lambda e, j=j: e.tensor_scalar(QD[:], QD[:], CB[:, j:j + 1], None, ALU.add),
                 r=[bconsts], w=[bQD])
            P.op("scalar", lambda e: e.activation(QT[:], QD[:], ACT.Sigmoid),
                 r=[bQD], w=[bQT])
            P.op("vector", lambda e: e.tensor_mul(QT[:], QT[:], QD[:]),
                 r=[bQD], w=[bQT])
            P.op("vector", lambda e, j=j: e.tensor_copy(BIG1[:, j * T:(j + 1) * T], QT[:]),
                 r=[bQT], w=[bBIG1])
            P.op("gpsimd", lambda e, j=j: e.tensor_tensor(BIG2[:, j * T:(j + 1) * T], QT[:],
                                                          BIG1[:, j * T:(j + 1) * T].bitcast(F32),
                                                          ALU.subtract),
                 r=[bQT, bBIG1], w=[bBIG2])
            P.op("scalar", lambda e, j=j, gb_=gb_: e.activation(SIGG[:, j * T:(j + 1) * T], PB[gb_][:],
                                                       ACT.Sigmoid, bias=BIN[:, KI + j:KI + j + 1]),
                 r=[bPB[gb_], bconsts], w=[bSIGG])

        for j in range(KI):
            bb_, cb_ = 2 + 2 * (j % 2), 3 + 2 * (j % 2)
            sl_a, sl_b = next_slot(), next_slot()
            pb1s, pb2s = POOL[sl_a][:], POOL[sl_b][:]
            P.dma("qw2", lambda e, j=j, d=pb1s: e.dma_start(out=d, in_=SB[0][j]), r=[bSBc[0]], w=[bPOOL[sl_a]])
            P.dma("qw2", lambda e, j=j, d=pb2s: e.dma_start(out=d, in_=SB[1][j]), r=[bSBc[1]], w=[bPOOL[sl_b]])
            fns = []
            for k in range(KI):
                st1 = pb1s[:, k * 128:(k + 1) * 128]
                st2 = pb2s[:, k * 128:(k + 1) * 128]
                m1 = BIG1[:, k * T:(k + 1) * T]
                m2 = BIG2[:, k * T:(k + 1) * T]
                fns.append(lambda e, st1=st1, m1=m1, k=k, bb_=bb_: e.matmul(
                    PB[bb_][:], st1, m1, start=(k == 0), stop=False))
                fns.append(lambda e, st1=st1, m2=m2, bb_=bb_: e.matmul(
                    PB[bb_][:], st1, m2, start=False, stop=False))
                fns.append(lambda e, st2=st2, m1=m1, k=k, bb_=bb_: e.matmul(
                    PB[bb_][:], st2, m1, start=False, stop=(k == KI - 1)))
            P.tgroup(fns, r=[bBIG1, bBIG2, bPOOL[sl_a], bPOOL[sl_b]], w=[bPB[bb_]])
            sl_c, sl_d = next_slot(), next_slot()
            pc1s, pc2s = POOL[sl_c][:], POOL[sl_d][:]
            P.dma("qw2", lambda e, j=j, d=pc1s: e.dma_start(out=d, in_=SB[0][KI + j]), r=[bSBc[0]], w=[bPOOL[sl_c]])
            P.dma("qw2", lambda e, j=j, d=pc2s: e.dma_start(out=d, in_=SB[1][KI + j]), r=[bSBc[1]], w=[bPOOL[sl_d]])
            fns = []
            for k in range(KI):
                st1 = pc1s[:, k * 128:(k + 1) * 128]
                st2 = pc2s[:, k * 128:(k + 1) * 128]
                m1 = BIG1[:, k * T:(k + 1) * T]
                m2 = BIG2[:, k * T:(k + 1) * T]
                fns.append(lambda e, st1=st1, m1=m1, k=k, cb_=cb_: e.matmul(
                    PB[cb_][:], st1, m1, start=(k == 0), stop=False))
                fns.append(lambda e, st1=st1, m2=m2, cb_=cb_: e.matmul(
                    PB[cb_][:], st1, m2, start=False, stop=False))
                fns.append(lambda e, st2=st2, m1=m1, k=k, cb_=cb_: e.matmul(
                    PB[cb_][:], st2, m1, start=False, stop=(k == KI - 1)))
            P.tgroup(fns, r=[bBIG1, bBIG2, bPOOL[sl_c], bPOOL[sl_d]], w=[bPB[cb_]])
            P.op("vector", lambda e, j=j, bb_=bb_: e.tensor_scalar(QT[:], PB[bb_][:], BBC[:, j:j + 1],
                                                          None, ALU.add),
                 r=[bPB[bb_], bconsts], w=[bQT])
            P.op("vector", lambda e, j=j, cb_=cb_: e.tensor_scalar(QAB[:], PB[cb_][:],
                                                          BBC[:, KI + j:KI + j + 1], None, ALU.add),
                 r=[bPB[cb_], bconsts], w=[bQAB])
            a_b = AD[:, j:j + 1].broadcast_to([128, T])
            init = 0.0 if (c_pos == 0) else SCAR[:, j:j + 1]
            extra_r = [bconsts] if c_pos == 0 else [bconsts, bSCAR]
            P.op("vector", lambda e, a_b=a_b, init=init: e.tensor_tensor_scan(
                QSG[:], a_b, QT[:], init, ALU.mult, ALU.add),
                 r=[bQT] + extra_r, w=[bQSG])
            P.op("vector", lambda e, j=j: e.tensor_copy(SCAR[:, j:j + 1], QSG[:, T - 1:T]),
                 r=[bQSG], w=[bSCAR])
            P.op("vector", lambda e: e.tensor_mul(QT[:], QSG[:], QAB[:]), r=[bQSG, bQAB], w=[bQT])
            P.op("vector", lambda e, j=j: e.tensor_tensor(QT[:], QT[:], SIGG[:, j * T:(j + 1) * T],
                                                          ALU.mult),
                 r=[bSIGG], w=[bQT])
            jq = j % SECT_Y
            P.op("vector", lambda e, jq=jq: e.tensor_copy(AQ1[:, jq * T:(jq + 1) * T], QT[:]),
                 r=[bQT], w=[bAQ1])
            P.op("gpsimd", lambda e, jq=jq: e.tensor_tensor(AQ2[:, jq * T:(jq + 1) * T], QT[:],
                                                            AQ1[:, jq * T:(jq + 1) * T].bitcast(F32),
                                                            ALU.subtract),
                 r=[bQT, bAQ1], w=[bAQ2])
            if jq == SECT_Y - 1:
                sec = j // SECT_Y
                for mo in range(KD):
                    sl = next_slot()
                    po1 = POOL[sl][:, 0:SECT_Y * 128]
                    po2 = POOL[sl][:, SECT_Y * 128:2 * SECT_Y * 128]
                    c0 = sec * SECT_Y * 128
                    P.dma("qw2", lambda e, mo=mo, c0=c0, d=po1: e.dma_start(
                        out=d, in_=SO[0][mo, :, c0:c0 + SECT_Y * 128]), r=[bSO[0]], w=[bPOOL[sl]])
                    P.dma("qw2", lambda e, mo=mo, c0=c0, d=po2: e.dma_start(
                        out=d, in_=SO[1][mo, :, c0:c0 + SECT_Y * 128]), r=[bSO[1]], w=[bPOOL[sl]])
                    pbi = 0 + (mo % 2)
                    fns = []
                    for jj in range(SECT_Y):
                        st1 = po1[:, jj * 128:(jj + 1) * 128]
                        st2 = po2[:, jj * 128:(jj + 1) * 128]
                        m1 = AQ1[:, jj * T:(jj + 1) * T]
                        m2 = AQ2[:, jj * T:(jj + 1) * T]
                        fns.append(lambda e, st1=st1, m1=m1, jj=jj, pbi=pbi: e.matmul(
                            PB[pbi][:], st1, m1, start=(jj == 0), stop=False))
                        fns.append(lambda e, st1=st1, m2=m2, pbi=pbi: e.matmul(
                            PB[pbi][:], st1, m2, start=False, stop=False))
                        fns.append(lambda e, st2=st2, m1=m1, jj=jj, pbi=pbi: e.matmul(
                            PB[pbi][:], st2, m1, start=False, stop=(jj == SECT_Y - 1)))
                    P.tgroup(fns, r=[bAQ1, bAQ2, bPOOL[sl]], w=[bPB[pbi]])
                    xc = X[:, mo * T:(mo + 1) * T]
                    if sec == NSEC_Y - 1:
                        P.op("vector", lambda e, mo=mo, pbi=pbi: e.tensor_scalar(
                            QC[:], PB[pbi][:], BOUT[:, mo:mo + 1], None, ALU.add),
                             r=[bPB[pbi], bconsts], w=[bQC])
                        P.op("vector", lambda e, xc=xc: e.tensor_add(xc, xc, QC[:]), r=[bQC], w=[bXc[mo]])
                    else:
                        P.op("vector", lambda e, xc=xc, pbi=pbi: e.scalar_tensor_tensor(
                            xc, PB[pbi][:], 1.0, xc, ALU.bypass, ALU.add), r=[bPB[pbi]], w=[bXc[mo]])

    out_tags = []
    for t in range(NT):
        s, c_pos = t // NT_SEQ, t % NT_SEQ
        for ch in range(KD):
            P.dma("qa", lambda e, s=s, c_pos=c_pos, ch=ch: e.dma_start(
                out=X[:, ch * T:(ch + 1) * T],
                in_=xin[s, ch * 128:(ch + 1) * 128, c_pos * T:(c_pos + 1) * T]),
                r=[bXc[ch]], w=[bXc[ch]])
        ffn_sublayer()
        if t == 0:
            prep_mm_weight(wb, SB[0], SB[1], bSBc[0], bSBc[1], KI, DBC, deferred=True)
            prep_mm_weight(wo, SO[0], SO[1], bSO[0], bSO[1], KI, D, deferred=True)
        mamba_sublayer(c_pos)
        ffn_sublayer()
        for ch in range(KD):
            tg = P.dma_s(lambda e, s=s, c_pos=c_pos, ch=ch: e.dma_start(
                out=yout[s, ch * 128:(ch + 1) * 128, c_pos * T:(c_pos + 1) * T],
                in_=X[:, ch * T:(ch + 1) * T]),
                r=[bXc[ch]], w=[B("yout")])
            out_tags.append(tg)

    P.sync_wait(out_tags)

    import contextlib
    with contextlib.ExitStack() as stack:
        for k in P.semkeys:
            P.sems[k] = stack.enter_context(nc.semaphore(k))
        with nc.Block() as block:
            @block.sync
            def _(eng):
                for f in P.ops["sync"]:
                    f(eng)

            @block.vector
            def _(eng):
                for f in P.ops["vector"]:
                    f(eng)

            @block.scalar
            def _(eng):
                for f in P.ops["scalar"]:
                    f(eng)

            @block.tensor
            def _(eng):
                for f in P.ops["tensor"]:
                    f(eng)

            @block.gpsimd
            def _(eng):
                for f in P.ops["gpsimd"]:
                    f(eng)
    return nc


_NC = None


def kernel(L0_attn_x, L0_attn_W_in, L0_attn_b_in, L0_attn_conv_w, L0_attn_conv_b,
           L0_attn_W_bc, L0_attn_b_bc, L0_attn_log_decay, L0_attn_W_out, L0_attn_b_out,
           L0_ffn_w_gate, L0_ffn_w_up, L0_ffn_w_down):
    global _NC
    if _NC is None:
        _NC = _build_nc()
    nc = _NC

    f32 = np.float32
    x = np.asarray(L0_attn_x, f32)
    com = {
        "wg": np.ascontiguousarray(np.asarray(L0_ffn_w_gate, f32)),
        "wu": np.ascontiguousarray(np.asarray(L0_ffn_w_up, f32)),
        "wd": np.ascontiguousarray(np.asarray(L0_ffn_w_down, f32)),
        "wi": np.ascontiguousarray(np.asarray(L0_attn_W_in, f32)),
        "wb": np.ascontiguousarray(np.asarray(L0_attn_W_bc, f32)),
        "wo": np.ascontiguousarray(np.asarray(L0_attn_W_out, f32)),
        "bin": np.ascontiguousarray(np.asarray(L0_attn_b_in, f32).reshape(KB, 128).T),
        "bbc": np.ascontiguousarray(np.asarray(L0_attn_b_bc, f32).reshape(KB, 128).T),
        "bout": np.ascontiguousarray(np.asarray(L0_attn_b_out, f32).reshape(KD, 128).T),
        "cwd": np.ascontiguousarray(np.asarray(L0_attn_conv_w, f32).reshape(KI, 128, 4).transpose(1, 2, 0)),
        "cbd": np.ascontiguousarray(np.asarray(L0_attn_conv_b, f32).reshape(KI, 128).T),
        "ldd": np.ascontiguousarray(np.asarray(L0_attn_log_decay, f32).reshape(KI, 128).T),
    }
    in_maps = []
    for i in range(8):
        xc = np.ascontiguousarray(x[2 * i:2 * i + 2].transpose(0, 2, 1))
        m = dict(com)
        m["x"] = xc
        in_maps.append(m)
    res = run_bass_kernel_spmd(nc, in_maps, core_ids=list(range(8)))
    outs = []
    for i in range(8):
        yc = np.asarray(res.results[i]["y"])
        outs.append(yc.transpose(0, 2, 1))
    return np.ascontiguousarray(np.concatenate(outs, axis=0))
